# version 29
# baseline (speedup 1.0000x reference)
import contextlib

import numpy as np
import ml_dtypes
import concourse.bass as bass
import concourse.mybir as mybir
from concourse.bass_utils import run_bass_kernel_spmd

F32 = mybir.dt.float32
BF16 = mybir.dt.bfloat16
AF = mybir.ActivationFunctionType
ALU = mybir.AluOpType
AX = mybir.AxisListType

B = 8
D = 1024
H = 16
DH = 64
C = 128
NE = 8
HID = 2730
HPAD = 2816
NHT = HPAD // 128
NDC = D // 128
ROPE_BASE = 10000.0
EPS = 1e-5


class Sem:
    def __init__(self, handle):
        self.h = handle
        self.n = 0

    def inc(self, instr, amount=1):
        instr.then_inc(self.h, amount)
        self.n += amount
        return self.n

    def wait(self, eng, n=None):
        eng.wait_ge(self.h, self.n if n is None else n)


class DStream:
    def __init__(self, mk, name):
        self.s = [mk(name + "a"), mk(name + "b")]

    def issue(self, eng, out, in_, idx):
        self.s[idx % 2].inc(eng.dma_start(out=out, in_=in_), 16)

    def wait(self, eng, idx):
        if idx < 0:
            return
        eng.wait_ge(self.s[idx % 2].h, 16 * (idx // 2 + 1))

    def wait_all(self, eng):
        eng.wait_ge(self.s[0].h, self.s[0].n)
        eng.wait_ge(self.s[1].h, self.s[1].n)


def build_core(T=4096, debug=False):
    NIT = T // 128
    NIB = T // 512
    N512 = T // 512
    NK = T // 256
    FIB = 1024 if T >= 1024 else T
    NFB = T // FIB
    FIT = FIB // 128
    NW = NHT // 2

    nc = bass.Bass("TRN2", target_bir_lowering=False)

    x_in = nc.dram_tensor("x", [T, D], F32, kind="ExternalInput").ap()
    w_dk = nc.dram_tensor("w_dk", [D, C], F32, kind="ExternalInput").ap()
    w_dv = nc.dram_tensor("w_dv", [D, C], F32, kind="ExternalInput").ap()
    w_uk = nc.dram_tensor("w_uk", [C, D], F32, kind="ExternalInput").ap()
    w_uvT = nc.dram_tensor("w_uvT", [D, C], F32, kind="ExternalInput").ap()
    w_o = nc.dram_tensor("w_o", [D, D], F32, kind="ExternalInput").ap()
    b_dk = nc.dram_tensor("b_dk", [C, 1], F32, kind="ExternalInput").ap()
    b_dv = nc.dram_tensor("b_dv", [C, 1], F32, kind="ExternalInput").ap()
    b_uk = nc.dram_tensor("b_uk", [128, NDC], F32, kind="ExternalInput").ap()
    b_uv = nc.dram_tensor("b_uv", [128, NDC], F32, kind="ExternalInput").ap()
    b_o = nc.dram_tensor("b_o", [1, D], F32, kind="ExternalInput").ap()
    wgT = nc.dram_tensor("wgT", [D, NE], F32, kind="ExternalInput").ap()
    w1t = nc.dram_tensor("w1t", [NE, NHT, NDC, 128, 128], F32, kind="ExternalInput").ap()
    w2t = nc.dram_tensor("w2t", [NE, NHT, NDC, 128, 128], F32, kind="ExternalInput").ap()
    w3t = nc.dram_tensor("w3t", [NE, NHT, 128, D], F32, kind="ExternalInput").ap()
    ident_in = nc.dram_tensor("ident", [128, 128], BF16, kind="ExternalInput").ap()
    rrot_in = nc.dram_tensor("rrot", [128, 128], BF16, kind="ExternalInput").ap()
    cos_in = nc.dram_tensor("cosT", [128, T], BF16, kind="ExternalInput").ap()
    sins_in = nc.dram_tensor("sinsT", [128, T], BF16, kind="ExternalInput").ap()
    onc_in = nc.dram_tensor("onc", [128, 1], BF16, kind="ExternalInput").ap()
    oncf_in = nc.dram_tensor("oncf", [128, 1], F32, kind="ExternalInput").ap()
    onr_in = nc.dram_tensor("onr", [1, 128], F32, kind="ExternalInput").ap()

    y_out = nc.dram_tensor("y", [T, D], F32, kind="ExternalOutput").ap()

    wt1_bf = nc.dram_tensor("wt1_bf", [NE, NHT, NDC, 128, 128], BF16, kind="Internal").ap()
    wt2_bf = nc.dram_tensor("wt2_bf", [NE, NHT, NDC, 128, 128], BF16, kind="Internal").ap()
    wt3_bf = nc.dram_tensor("wt3_bf", [NE, NHT, 128, D], BF16, kind="Internal").ap()
    ln1t_scr = nc.dram_tensor("ln1t_scr", [NDC, 128, T], BF16, kind="Internal").ap()
    kt_scr = nc.dram_tensor("kt_scr", [NDC, 128, T], BF16, kind="Internal").ap()
    ln2t_scr = nc.dram_tensor("ln2t_scr", [T, D], BF16, kind="Internal").ap()
    x1_scr = nc.dram_tensor("x1_scr", [T, D], F32, kind="Internal").ap()
    wvo_scr = nc.dram_tensor("wvo_scr", [H, C, D], BF16, kind="Internal").ap()

    dbg = {}
    if debug:
        for nm, shp, dt in [("ln1t", [NDC, 128, T], BF16), ("ckt", [C, T], BF16),
                            ("cv", [128, T], BF16), ("kt", [NDC, 128, T], BF16),
                            ("x1", [T, D], F32), ("g", [NIT, 128, NE], F32),
                            ("ln2t", [T, D], BF16), ("negm", [128, NDC], F32),
                            ("dacc", [NIB, 128, H * 512], F32)]:
            dbg[nm] = nc.dram_tensor("dbg_" + nm, shp, dt, kind="ExternalOutput").ap()

    es = contextlib.ExitStack()
    with es:
        def sem(name):
            return Sem(es.enter_context(nc.semaphore(name)))

        ts, vs, ss = sem("ts"), sem("vs"), sem("ss")
        s_const = sem("s_const")
        g_attnw = sem("g_attnw")
        g_w = sem("g_w")
        s_wvow = sem("s_wvow")
        s_ln1w = DStream(sem, "ln1w")
        s_ktw = sem("s_ktw")
        s_x1 = DStream(sem, "x1ld")
        s_kt = DStream(sem, "ktld")
        s_ln1r = DStream(sem, "ln1r")
        s_wvo = sem("s_wvo")
        s_x2 = DStream(sem, "x2ld")
        s_x1w = DStream(sem, "x1w")
        s_ln2w = DStream(sem, "ln2w")
        s_w3 = sem("s_w3")
        s_w12 = DStream(sem, "w12")
        s_ln2r = sem("s_ln2r")
        s_x1r = DStream(sem, "x1r")
        s_y = DStream(sem, "yst")
        s_dbg = sem("s_dbg")

        blk = es.enter_context(nc.Block())

        def sbo(name, shp, dt=F32):
            return es.enter_context(nc.sbuf_tensor(name, shp, dt))[:, :]

        onc = sbo("onc_s", [128, 1], BF16)
        oncf = sbo("oncf_s", [128, 1], F32)
        onr = sbo("onr_s", [1, 128], F32)
        onrb = sbo("onrb_s", [1, 128], BF16)
        epst = sbo("epst", [128, 1], F32)
        identb = sbo("identb", [128, 128], BF16)
        gbuf = sbo("gbuf", [128, NIT * NE], F32)
        wgT_res = sbo("wgT_res", [128, NDC * NE], BF16)

        sca = contextlib.ExitStack()
        with sca:
            def sba(name, shp, dt=F32):
                return sca.enter_context(nc.sbuf_tensor(name, shp, dt))[:, :]

            rrotb = sba("rrotb", [128, 128], BF16)
            cosT = sba("cosT_s", [128, T], BF16)
            sinsT = sba("sinsT_s", [128, T], BF16)
            wdk_sb = sba("wdk_sb", [128, NDC * 128], BF16)
            wdv_sb = sba("wdv_sb", [128, NDC * 128], BF16)
            wuk_sb = sba("wuk_sb", [128, D], BF16)

            bdk_sb = sba("bdk_sb", [128, 1], F32)
            bdv_sb = sba("bdv_sb", [128, 1], F32)
            buk_sb = sba("buk_sb", [128, NDC], F32)
            cbias_bc = sba("cbias_bc", [128, D], F32)
            negM = sba("negM", [128, NDC], F32)
            cv_sb = sba("cv_sb", [128, T], BF16)
            ktp = [sba(f"ktp{i}", [128, T], BF16) for i in range(2)]

            @blk.sync
            def _(sync):
                for t_, d_ in [(identb, ident_in), (rrotb, rrot_in), (cosT, cos_in),
                               (sinsT, sins_in), (onc, onc_in), (oncf, oncf_in),
                               (onr, onr_in), (bdk_sb, b_dk), (bdv_sb, b_dv)]:
                    s_const.inc(sync.dma_start(out=t_, in_=d_), 16)
                s_const.inc(sync.dma_start(out=buk_sb, in_=b_uk), 16)
            m_const = s_const.n

            @blk.gpsimd
            def _(gpsimd):
                g_attnw.inc(gpsimd.dma_start(
                    out=wdk_sb[:, :].rearrange("p (c f) -> p c f", c=NDC),
                    in_=w_dk.rearrange("(c p) f -> p c f", p=128)), 16)
                g_attnw.inc(gpsimd.dma_start(
                    out=wdv_sb[:, :].rearrange("p (c f) -> p c f", c=NDC),
                    in_=w_dv.rearrange("(c p) f -> p c f", p=128)), 16)
                g_attnw.inc(gpsimd.dma_start(out=wuk_sb, in_=w_uk), 16)
                g_attnw.inc(gpsimd.dma_start(
                    out=wgT_res[:, :].rearrange("p (c f) -> p c f", c=NDC),
                    in_=wgT.rearrange("(c p) f -> p c f", p=128)), 16)
            m_attnw1 = g_attnw.n

            with contextlib.ExitStack() as ph0:
                def sb0(name, shp, dt=F32):
                    return ph0.enter_context(nc.sbuf_tensor(name, shp, dt))[:, :]
                wuvT_sb = sb0("wuvT_sb", [128, NDC * 128], BF16)
                wo_sb = sb0("wo_sb", [128, NDC * D], BF16)
                buv_f = sb0("buv_f", [128, NDC], F32)
                buv_bf = sb0("buv_bf", [128, NDC], BF16)
                wvo_st = sb0("wvo_st", [128, D], BF16)
                crow = sb0("crow", [1, D], F32)
                crow2 = sb0("crow2", [1, D], F32)
                bo_sb = sb0("bo_sb", [1, D], F32)
                pW = ph0.enter_context(nc.psum_tensor("pW", [128, 1024], F32))[:, :]
                pCB = ph0.enter_context(nc.psum_tensor("pCB", [1, 1024], F32))[:, :]

                @blk.sync
                def _(sync):
                    s_const.inc(sync.dma_start(out=buv_f, in_=b_uv), 16)
                    s_const.inc(sync.dma_start(out=bo_sb, in_=b_o), 16)
                m_const2 = s_const.n

                @blk.gpsimd
                def _(gpsimd):
                    g_attnw.inc(gpsimd.dma_start(
                        out=wuvT_sb[:, :].rearrange("p (c f) -> p c f", c=NDC),
                        in_=w_uvT.rearrange("(c p) f -> p c f", p=128)), 16)
                    g_attnw.inc(gpsimd.dma_start(
                        out=wo_sb[:, :].rearrange("p (c f) -> p c f", c=NDC),
                        in_=w_o.rearrange("(c p) f -> p c f", p=128)), 16)
                m_attnw = g_attnw.n

                @blk.vector
                def _(vector):
                    vector.wait_ge(s_const.h, m_const2)
                    vs.inc(vector.memset(epst, EPS))
                    vs.inc(vector.memset(onrb, 1.0))
                    vs.inc(vector.tensor_copy(buv_bf, buv_f))
                m_buv = vs.n

                wvo_ev = {-1: 0}
                for h in range(H):
                    dc, r0 = h // 2, (h % 2) * 64

                    @blk.tensor
                    def _(tensor, h=h, dc=dc, r0=r0):
                        if h == 0:
                            tensor.wait_ge(g_attnw.h, m_attnw)
                        tensor.wait_ge(vs.h, wvo_ev[h - 1])
                        for nt in range(2):
                            mm = tensor.matmul(
                                pW[:, nt * 512:(nt + 1) * 512],
                                wuvT_sb[r0:r0 + 64, dc * 128:(dc + 1) * 128],
                                wo_sb[r0:r0 + 64, dc * D + nt * 512: dc * D + (nt + 1) * 512],
                                start=True, stop=True)
                        ts.inc(mm)
                    m_wmm = ts.n

                    @blk.vector
                    def _(vector, h=h, m_wmm=m_wmm):
                        vector.wait_ge(ts.h, m_wmm)
                        if h >= 1:
                            vector.wait_ge(s_wvow.h, 16 * h)
                        vs.inc(vector.tensor_copy(wvo_st, pW))
                    wvo_ev[h] = vs.n

                    @blk.sync
                    def _(sync, h=h):
                        sync.wait_ge(vs.h, wvo_ev[h])
                        s_wvow.inc(sync.dma_start(out=wvo_scr[h], in_=wvo_st), 16)

                @blk.tensor
                def _(tensor):
                    tensor.wait_ge(vs.h, m_buv)
                    for nt in range(2):
                        for dc in range(NDC):
                            mm = tensor.matmul(
                                pCB[0:1, nt * 512:(nt + 1) * 512],
                                buv_bf[:, dc:dc + 1],
                                wo_sb[:, dc * D + nt * 512: dc * D + (nt + 1) * 512],
                                start=(dc == 0), stop=(dc == NDC - 1))
                    ts.inc(mm)
                m_cb = ts.n

                @blk.scalar
                def _(scalar):
                    scalar.wait_ge(ts.h, m_cb)
                    ss.inc(scalar.copy(crow, pCB))
                m_cr = ss.n

                @blk.vector
                def _(vector):
                    vector.wait_ge(ss.h, m_cr)
                    vs.inc(vector.tensor_tensor(crow2, crow, bo_sb, ALU.add))
                m_cr2 = vs.n

                @blk.tensor
                def _(tensor):
                    tensor.wait_ge(vs.h, m_cr2)
                    tensor.wait_ge(vs.h, wvo_ev[H - 1])
                    ts.inc(tensor.matmul(pW[:, 0:512], onr, crow2[:, 0:512],
                                         start=True, stop=True))
                    ts.inc(tensor.matmul(pW[:, 512:1024], onr, crow2[:, 512:1024],
                                         start=True, stop=True))
                m_cbc = ts.n

                @blk.scalar
                def _(scalar):
                    scalar.wait_ge(ts.h, m_cbc)
                    scalar.wait_ge(s_wvow.h, 16 * H)
                    ss.inc(scalar.copy(cbias_bc, pW))

            @blk.gpsimd
            def _(gpsimd):
                gpsimd.wait_ge(s_ktw.h, 16 * NDC)
                NLOAD = NIB * NDC - 2
                ci = 0
                for e in range(NE):
                    for dst, src in ((wt1_bf, w1t), (wt2_bf, w2t), (wt3_bf, w3t)):
                        for h0, h1 in ((0, NHT // 2), (NHT // 2, NHT)):
                            idx = min(ci, NLOAD - 1)
                            gpsimd.wait_ge(s_kt.s[idx % 2].h, 16 * (idx // 2 + 1))
                            g_w.inc(gpsimd.dma_start(out=dst[e, h0:h1],
                                                     in_=src[e, h0:h1]), 16)
                            ci += 1
            m_wcast = g_w.n

            m_ph0_t, m_ph0_v, m_ph0_s = ts.n, vs.n, ss.n
            m_const_all = s_const.n

            with contextlib.ExitStack() as ph1:
                def sb1(name, shp, dt=F32):
                    return ph1.enter_context(nc.sbuf_tensor(name, shp, dt))[:, :]
                xt = [sb1(f"xt{i}", [128, D]) for i in range(4)]
                stats = [sb1(f"stats{i}", [128, 12]) for i in range(2)]
                mv = [sb1(f"mv{i}", [128, 2]) for i in range(2)]
                lnv = [sb1(f"lnv{i}", [128, 1]) for i in range(2)]
                rstd = [sb1(f"rstd{i}", [128, 1]) for i in range(2)]
                lnt = [sb1(f"lnt{i}", [128, D], BF16) for i in range(2)]
                tstage = [sb1(f"tstage{i}", [128, NDC * 128], BF16) for i in range(2)]
                ckT = sb1("ckT", [128, T], BF16)
                cvT_st = [sb1(f"cvT_st{i}", [128, 128], BF16) for i in range(2)]
                ktmp_bf = sb1("ktmp_bf", [128, 512], BF16)
                ropeA = sb1("ropeA", [128, 512], BF16)
                ropeB = sb1("ropeB", [128, 512], BF16)
                sqf = sb1("sqf", [128, T], BF16)
                mxt = sb1("mxt", [1, N512])
                mxs = sb1("mxs", [1, 1])
                mlog = sb1("mlog", [1, 1])
                mexp = sb1("mexp", [1, 1])
                mneg = sb1("mneg", [1, 1])
                pT = ph1.enter_context(nc.psum_tensor("pT", [128, 1024], F32))[:, :]
                pSm = ph1.enter_context(nc.psum_tensor("pSm", [128, 512], F32))[:, :]
                pCk = pSm[:, 0:128]
                pCv = pSm[:, 128:256]
                pBc = pSm[:, 256:257]
                pCvT = ph1.enter_context(nc.psum_tensor("pCvT", [128, 128], F32))[:, :]
                pK = ph1.enter_context(nc.psum_tensor("pK", [128, 1024], F32))[:, :]
                pKR = ph1.enter_context(nc.psum_tensor("pKR", [128, 512], F32))[:, :]
                pMx = ph1.enter_context(nc.psum_tensor("pMx", [1, 512], F32))[:, :]

                m_agg = {-1: 0}
                m_rstdd = {-2: 0, -1: 0}
                m_norm = {-4: 0, -3: 0, -2: 0, -1: 0}
                m_tr = {-2: 0, -1: 0}
                m_stage = {-2: 0, -1: 0}
                m_cmm = {-2: 0, -1: 0}
                m_cev = {-2: 0, -1: 0}
                m_cvtr = {-2: 0, -1: 0}
                m_cvev = {-2: 0, -1: 0}

                def emit_ln_a(it):
                    p = it % 2
                    p4 = it % 4
                    q = it % 2

                    @blk.sync
                    def _(sync, it=it, p4=p4):
                        if it < 4:
                            sync.wait_ge(ts.h, m_ph0_t)
                            sync.wait_ge(vs.h, m_ph0_v)
                            sync.wait_ge(ss.h, m_ph0_s)
                        if it >= 4:
                            sync.wait_ge(vs.h, m_norm[it - 4])
                        s_x1.issue(sync, xt[p4], x_in[it * 128:(it + 1) * 128, :], it)

                    @blk.vector
                    def _(vector, it=it, p4=p4, q=q):
                        s_x1.wait(vector, it)
                        vector.wait_ge(vs.h, m_norm[it - 2])
                        vector.wait_ge(ss.h, m_rstdd[it - 2])
                        vector.bn_stats(stats[q][:, 0:6], xt[p4][:, 0:512])
                        vs.inc(vector.bn_stats(stats[q][:, 6:12], xt[p4][:, 512:1024]))
                        vs.wait(vector)
                        vs.inc(vector.bn_aggr(mv[q], stats[q]))
                    m_agg[it] = vs.n

                    @blk.scalar
                    def _(scalar, it=it, q=q):
                        scalar.wait_ge(vs.h, m_agg[it])
                        scalar.wait_ge(vs.h, m_norm[it - 2])
                        ss.inc(scalar.activation(lnv[q], mv[q][:, 1:2], AF.Ln,
                                                 bias=epst, scale=1.0))
                        ss.wait(scalar)
                        ss.inc(scalar.activation(rstd[q], lnv[q], AF.Exp, scale=-0.5))
                    m_rstdd[it] = ss.n

                    @blk.vector
                    def _(vector, it=it, p=p, p4=p4, q=q):
                        vector.wait_ge(ss.h, m_rstdd[it])
                        vector.wait_ge(ts.h, m_tr.get(it - 2, 0))
                        vs.inc(vector.tensor_scalar(lnt[p], xt[p4], mv[q][:, 0:1],
                                                    rstd[q],
                                                    ALU.subtract, ALU.mult))
                    m_norm[it] = vs.n

                pT_x = [pT, pK]
                pCk_x = [pCk, pKR[:, 0:128]]
                pCv_x = [pCv, pKR[:, 128:256]]
                pCvT_x = [pCvT, pKR[:, 256:384]]

                def emit_ln_b(it):
                    p = it % 2
                    q = it % 2

                    @blk.tensor
                    def _(tensor, it=it, p=p, q=q):
                        tensor.wait_ge(vs.h, m_norm[it])
                        if it == 0:
                            tensor.wait_ge(s_const.h, m_const_all)
                        tensor.wait_ge(vs.h, m_stage[it - 2])
                        for dc in range(NDC):
                            mm = tensor.matmul(pT_x[q][:, dc * 128:(dc + 1) * 128],
                                               lnt[p][:, dc * 128:(dc + 1) * 128], identb,
                                               start=True, stop=True)
                        ts.inc(mm)
                    m_tr[it] = ts.n

                    @blk.vector
                    def _(vector, it=it, p=p, q=q):
                        vector.wait_ge(ts.h, m_tr[it])
                        vector.wait_ge(ts.h, m_cmm[it - 2])
                        s_ln1w.wait(vector, it - 2)
                        vs.inc(vector.tensor_copy(tstage[p], pT_x[q]))
                    m_stage[it] = vs.n

                    @blk.tensor
                    def _(tensor, it=it, p=p, q=q):
                        tensor.wait_ge(vs.h, m_stage[it])
                        if it == 0:
                            tensor.wait_ge(g_attnw.h, m_attnw)
                        tensor.wait_ge(vs.h, m_cev[it - 2])
                        for dc in range(NDC):
                            tensor.matmul(pCk_x[q], wdk_sb[:, dc * 128:(dc + 1) * 128],
                                          tstage[p][:, dc * 128:(dc + 1) * 128],
                                          start=(dc == 0), stop=(dc == NDC - 1))
                        for dc in range(NDC):
                            mm = tensor.matmul(pCv_x[q], wdv_sb[:, dc * 128:(dc + 1) * 128],
                                               tstage[p][:, dc * 128:(dc + 1) * 128],
                                               start=(dc == 0), stop=(dc == NDC - 1))
                        ts.inc(mm)
                    m_cmm[it] = ts.n

                    @blk.vector
                    def _(vector, it=it, q=q):
                        vector.wait_ge(ts.h, m_cmm[it])
                        vector.wait_ge(ts.h, m_cvtr[it - 2])
                        vs.inc(vector.tensor_scalar(ckT[:, it * 128:(it + 1) * 128],
                                                    pCk_x[q], bdk_sb, None, ALU.add))
                        vs.inc(vector.tensor_scalar(cvT_st[q], pCv_x[q],
                                                    bdv_sb, None, ALU.add))
                    m_cev[it] = vs.n

                    @blk.tensor
                    def _(tensor, it=it, q=q):
                        tensor.wait_ge(vs.h, m_cev[it])
                        tensor.wait_ge(vs.h, m_cvev[it - 2])
                        ts.inc(tensor.matmul(pCvT_x[q], cvT_st[q], identb,
                                             start=True, stop=True))
                    m_cvtr[it] = ts.n

                    @blk.vector
                    def _(vector, it=it, q=q):
                        vector.wait_ge(ts.h, m_cvtr[it])
                        vs.inc(vector.tensor_copy(cv_sb[:, it * 128:(it + 1) * 128],
                                                  pCvT_x[q]))
                    m_cvev[it] = vs.n

                    @blk.sync
                    def _(sync, it=it, p=p):
                        sync.wait_ge(vs.h, m_stage[it])
                        s_ln1w.issue(
                            sync,
                            ln1t_scr.rearrange("c p t -> p c t")[:, :, it * 128:(it + 1) * 128],
                            tstage[p][:, :].rearrange("p (c f) -> p c f", c=NDC), it)


                for it in range(NIT):
                    emit_ln_a(it)
                    if it >= 1:
                        emit_ln_b(it - 1)
                emit_ln_b(NIT - 1)

                m_ck_all = m_cev[NIT - 1]
                m_kbf = {-2: 0, -1: 0}
                m_krope = {-1: 0}
                m_krotmm = {-1: 0}
                m_pmx_free = 0
                m_pbc_free = 0
                m_sq_readers = 0

                for jp in range(NDC):
                    pr = (jp + 2) % NDC
                    for it2 in range(N512):
                        i0 = it2 * 512
                        gk = jp * N512 + it2
                        q = it2 % 2

                        @blk.tensor
                        def _(tensor, pr=pr, i0=i0, q=q, gk=gk):
                            if gk == 0:
                                tensor.wait_ge(vs.h, m_ck_all)
                            tensor.wait_ge(vs.h, m_kbf[gk - 2])
                            ts.inc(tensor.matmul(pK[:, q * 512:(q + 1) * 512],
                                                 wuk_sb[:, pr * 128:(pr + 1) * 128],
                                                 ckT[:, i0:i0 + 512],
                                                 start=True, stop=True))
                        m_kmm = ts.n

                        @blk.vector
                        def _(vector, pr=pr, q=q, gk=gk, m_kmm=m_kmm):
                            vector.wait_ge(ts.h, m_kmm)
                            vector.wait_ge(ts.h, m_krotmm[gk - 1])
                            vs.inc(vector.tensor_scalar(ktmp_bf, pK[:, q * 512:(q + 1) * 512],
                                                        buk_sb[:, pr:pr + 1], None, ALU.add))
                        m_kbf[gk] = vs.n

                        @blk.tensor
                        def _(tensor, gk=gk):
                            tensor.wait_ge(vs.h, m_kbf[gk])
                            tensor.wait_ge(vs.h, m_krope[gk - 1])
                            ts.inc(tensor.matmul(pKR, rrotb, ktmp_bf, start=True, stop=True))
                        m_krotmm[gk] = ts.n

                        @blk.vector
                        def _(vector, i0=i0, gk=gk, pr=pr, it2=it2, jp=jp):
                            vector.wait_ge(ts.h, m_krotmm[gk])
                            if it2 == 0 and jp >= 2:
                                vector.wait_ge(s_ktw.h, 16 * (jp - 1))
                            vs.inc(vector.tensor_tensor(ropeA, ktmp_bf,
                                                        cosT[:, i0:i0 + 512], ALU.mult))
                            vs.inc(vector.tensor_tensor(ropeB, pKR,
                                                        sinsT[:, i0:i0 + 512], ALU.mult))
                            vs.wait(vector)
                            vs.inc(vector.tensor_tensor(ktp[pr % 2][:, i0:i0 + 512],
                                                        ropeA, ropeB, ALU.add))
                        m_krope[gk] = vs.n

                    @blk.vector
                    def _(vector, pr=pr, jp=jp):
                        vector.wait_ge(vs.h, m_krope[jp * N512 + N512 - 1])
                        vector.wait_ge(ts.h, m_sq_readers)
                        vs.inc(vector.tensor_tensor(sqf, ktp[pr % 2],
                                                    ktp[pr % 2], ALU.mult))
                    m_sq = vs.n

                    for it2 in range(N512):
                        @blk.tensor
                        def _(tensor, it2=it2, m_sq=m_sq):
                            tensor.wait_ge(vs.h, m_sq)
                            tensor.wait_ge(vs.h, m_pmx_free)
                            ts.inc(tensor.matmul(pMx, onc,
                                                 sqf[:, it2 * 512:(it2 + 1) * 512],
                                                 start=True, stop=True))
                        m_mmx = ts.n
                        m_sq_readers = ts.n

                        @blk.vector
                        def _(vector, it2=it2, m_mmx=m_mmx):
                            vector.wait_ge(ts.h, m_mmx)
                            vs.inc(vector.reduce_max(mxt[0:1, it2:it2 + 1], pMx, axis=AX.X))
                        m_pmx_free = vs.n

                    @blk.vector
                    def _(vector):
                        vector.wait_ge(vs.h, vs.n)
                        vs.inc(vector.reduce_max(mxs, mxt, axis=AX.X))
                    m_mxs = vs.n

                    @blk.scalar
                    def _(scalar, m_mxs=m_mxs):
                        scalar.wait_ge(vs.h, m_mxs)
                        ss.inc(scalar.activation(mexp, mxs, AF.Sqrt))
                    m_me = ss.n

                    @blk.vector
                    def _(vector, m_me=m_me):
                        vector.wait_ge(ss.h, m_me)
                        vs.inc(vector.tensor_scalar(mneg, mexp, -4.0, None, ALU.mult))
                    m_mn = vs.n

                    @blk.tensor
                    def _(tensor, m_mn=m_mn):
                        tensor.wait_ge(vs.h, m_mn)
                        tensor.wait_ge(ss.h, m_pbc_free)
                        ts.inc(tensor.matmul(pBc, onr, mneg, start=True, stop=True))
                    m_mb = ts.n

                    @blk.scalar
                    def _(scalar, pr=pr, m_mb=m_mb):
                        scalar.wait_ge(ts.h, m_mb)
                        ss.inc(scalar.copy(negM[:, pr:pr + 1], pBc))
                    m_pbc_free = ss.n

                    @blk.sync
                    def _(sync, pr=pr, m_sq=m_sq):
                        sync.wait_ge(vs.h, m_sq)
                        s_ktw.inc(sync.dma_start(out=kt_scr[pr], in_=ktp[pr % 2]), 16)

                m_negm_all = ss.n

                if debug:
                    @blk.sync
                    def _(sync):
                        s_ln1w.wait_all(sync)
                        sync.wait_ge(vs.h, vs.n)
                        sync.wait_ge(ss.h, ss.n)
                        sync.wait_ge(ts.h, ts.n)
                        s_dbg.inc(sync.dma_start(out=dbg["ln1t"], in_=ln1t_scr), 16)
                        s_dbg.inc(sync.dma_start(out=dbg["kt"], in_=kt_scr), 16)
                        s_dbg.inc(sync.dma_start(out=dbg["negm"], in_=negM), 16)
                        s_dbg.inc(sync.dma_start(out=dbg["ckt"], in_=ckT), 16)
                        s_dbg.inc(sync.dma_start(out=dbg["cv"], in_=cv_sb), 16)
                        sync.wait_ge(s_dbg.h, s_dbg.n)

            m_ph1_t, m_ph1_v, m_ph1_s = ts.n, vs.n, ss.n

            with contextlib.ExitStack() as ph2:
                def sb2(name, shp, dt=F32):
                    return ph2.enter_context(nc.sbuf_tensor(name, shp, dt))[:, :]
                lnsl = [sb2(f"lnsl{i}", [128, 512], BF16) for i in range(2)]
                qtp = sb2("qtp", [128, 512], BF16)
                qrA = sb2("qrA", [128, 512], BF16)
                qrB = sb2("qrB", [128, 512], BF16)
                NEXP = 6
                expb = [sb2(f"expb{i}", [128, 1024], BF16) for i in range(NEXP)]
                ts2 = [sb2(f"ts2_{i}", [128, 1024], BF16) for i in range(2)]
                t4 = [sb2(f"t4_{i}", [128, 1024], BF16) for i in range(2)]
                dacc = sb2("dacc", [128, H * 512])
                u_raw = sb2("u_raw", [128, H * 512], BF16)
                wvo_sb = sb2("wvo_sb", [128, H * 1024], BF16)
                dlgr = sb2("dlgr", [1, 1024])
                dinvr = sb2("dinvr", [1, 2048], BF16)
                xp2 = [sb2(f"xp2{i}", [128, D]) for i in range(2)]
                x1t = [sb2(f"x1t{i}", [128, D]) for i in range(2)]
                x1s = sb2("x1s", [128, D])
                stats2 = sb2("stats2", [128, 12])
                mv2 = sb2("mv2", [128, 2])
                lnv2 = sb2("lnv2", [128, 1])
                rstd2 = sb2("rstd2", [128, 1])
                ln2tile = [sb2(f"ln2tile{i}", [128, D], BF16) for i in range(2)]

                pSr = [ph2.enter_context(nc.psum_tensor(f"pSr{i}", [128, 1024], F32))[:, :]
                       for i in range(3)]
                pS = pSr
                pU = ph2.enter_context(nc.psum_tensor("pU", [128, 512], F32))[:, :]
                pU2 = ph2.enter_context(nc.psum_tensor("pU2", [128, 512], F32))[:, :]
                pSA = pSr[0]
                pSB = pSr[1]
                pQR = pSr[2][:, 0:512]
                pMS = pSr[2][:, 512:1024]

                @blk.sync
                def _(sync):
                    sync.wait_ge(ts.h, m_ph1_t)
                    sync.wait_ge(vs.h, m_ph1_v)
                    sync.wait_ge(ss.h, m_ph1_s)
                    sync.wait_ge(s_wvow.h, 16 * H)
                    s_wvo.inc(sync.dma_start(
                        out=wvo_sb[:, :].rearrange("p (h f) -> p h f", h=H),
                        in_=wvo_scr.rearrange("h p f -> p h f")), 16)
                m_wvold = s_wvo.n

                m_qklast = {-2: 0, -1: 0}
                m_exp_end = {-1: 0}
                m_qrope = {-1: 0}
                m_uev = {-1: 0}
                m_dacc_end = {}
                m_p4v = {-1: 0}
                m_expev = {}
                m_umm = {}
                m_dacc = {}
                m_attnev = {-1: 0}
                m_attnev_ps = {-1: 0}
                m_rstd2 = {-1: 0}
                m_norm2 = {-2: 0, -1: 0}
                m_dscale_prev = {}
                m_p4t = {-2: 0, -1: 0}
                m_yv = {}

                def issue_pair_loads(sync, gp):
                    ib_, pr_ = gp // NDC, gp % NDC
                    sync.wait_ge(ts.h, m_qklast[gp - 2])
                    if gp >= 2:
                        sync.wait_ge(s_ktw.h, 16 * NDC)
                        s_kt.issue(sync, ktp[gp % 2], kt_scr[pr_], gp - 2)
                    s_ln1r.issue(sync, lnsl[gp % 2],
                                 ln1t_scr[pr_][:, ib_ * 512:(ib_ + 1) * 512], gp)

                @blk.sync
                def _(sync):
                    s_ln1w.wait_all(sync)
                    issue_pair_loads(sync, 0)
                    if NIB * NDC > 1:
                        issue_pair_loads(sync, 1)

                m_qrot = {}

                def emit_qrot(gp):
                    bp = gp % 2

                    @blk.tensor
                    def _(tensor, gp=gp, bp=bp):
                        s_ln1r.wait(tensor, gp)
                        tensor.wait_ge(vs.h, m_qrope[gp - 1])
                        tensor.wait_ge(ss.h, m_exp_end[gp - 1])
                        ts.inc(tensor.matmul(pQR, rrotb, lnsl[bp],
                                             start=True, stop=True))
                    m_qrot[gp] = ts.n

                def emit_qrope(gp):
                    bp = gp % 2
                    ib_ = gp // NDC

                    @blk.vector
                    def _(vector, ib_=ib_, gp=gp, bp=bp):
                        vector.wait_ge(ts.h, m_qrot[gp])
                        vector.wait_ge(ts.h, m_qklast[gp - 1])
                        i0 = ib_ * 512
                        vs.inc(vector.tensor_tensor(qrA, lnsl[bp],
                                                    cosT[:, i0:i0 + 512], ALU.mult))
                        vs.inc(vector.tensor_tensor(qrB, pQR,
                                                    sinsT[:, i0:i0 + 512], ALU.mult))
                        vs.wait(vector)
                        vs.inc(vector.tensor_tensor(qtp, qrA, qrB, ALU.add))
                    m_qrope[gp] = vs.n

                TP = 2 * (T // 128) + 1
                VP = (T // 128) + 5

                def free_inc(vs_b, k):
                    g, r = k // 8, k % 8
                    return vs_b + 8 * g + (1, 1, 2, 2, 4, 4, 5, 5)[r]

                for ib in range(NIB):
                    ib_ts, ib_ss, ib_vs = ts.n, ss.n, vs.n

                    @blk.scalar
                    def _(scalar, ib=ib, ib_ts=ib_ts, ib_vs=ib_vs):
                        NJC = T // 128
                        for pr in range(NDC):
                            ts_b = ib_ts + pr * TP + 1
                            vs_b = ib_vs + pr * VP + 3
                            for k in range(NJC):
                                qkc = k + 1 if k < 2 else 2 * k - 1
                                scalar.wait_ge(ts.h, ts_b + qkc)
                                if k >= NEXP:
                                    scalar.wait_ge(vs.h, free_inc(vs_b, k - NEXP))
                                ss.inc(scalar.activation(expb[k % NEXP], pSr[k % 3],
                                                         AF.Exp,
                                                         bias=negM[:, pr:pr + 1],
                                                         scale=0.125))

                    for pr in range(NDC):
                        gp = ib * NDC + pr
                        bp = gp % 2
                        assert ts.n == ib_ts + pr * TP + (1 if pr else 0)
                        assert vs.n == ib_vs + (pr * VP + 3 if pr else 0)

                        if pr == 0:
                            emit_qrot(gp)
                            emit_qrope(gp)

                        NJC = T // 128
                        ts_b, vs_b = ts.n, vs.n
                        ss_b = ib_ss + pr * NJC

                        m_exp_end[gp] = ss_b + NJC

                        for k in range(NJC):
                            @blk.tensor
                            def _(tensor, gp=gp, bp=bp, k=k, ib=ib, pr=pr, ss_b=ss_b):
                                if k == 0:
                                    s_kt.wait(tensor, gp - 2)
                                    tensor.wait_ge(vs.h, m_qrope[gp])
                                    if gp == 0:
                                        tensor.wait_ge(ss.h, m_negm_all)
                                    if pr == 0 and ib > 0:
                                        tensor.wait_ge(vs.h, m_attnev_ps[(ib - 1) * 4 + 2])
                                if k == 1 and pr == 0 and ib > 0:
                                    tensor.wait_ge(vs.h, m_attnev_ps[(ib - 1) * 4 + 3])
                                if k == 2 and pr == 0 and ib > 0:
                                    tensor.wait_ge(vs.h, m_dscale_prev[ib - 1])
                                if k >= 3:
                                    tensor.wait_ge(ss.h, ss_b + k - 2)
                                for half in range(2):
                                    r0 = half * 64
                                    mm = tensor.matmul(
                                        pSr[k % 3][:, half * 512:(half + 1) * 512],
                                        ktp[bp][r0:r0 + 64, k * 128:(k + 1) * 128],
                                        qtp[r0:r0 + 64, :],
                                        start=True, stop=True,
                                        tile_position=(r0, 0))
                                ts.inc(mm)
                                if k >= 2:
                                    kk = k - 2
                                    tensor.wait_ge(ss.h, ss_b + kk + 1)
                                    if kk == 0:
                                        tensor.wait_ge(vs.h, m_uev[gp - 1])
                                        pass
                                    for half, pUx in ((0, pU), (1, pU2)):
                                        mm = tensor.matmul(
                                            pUx, cv_sb[:, kk * 128:(kk + 1) * 128],
                                            expb[kk % NEXP][:, half * 512:(half + 1) * 512],
                                            start=(kk == 0), stop=False)
                                    ts.inc(mm)

                        @blk.tensor
                        def _(tensor, gp=gp, ss_b=ss_b):
                            for kk in (NJC - 2, NJC - 1):
                                tensor.wait_ge(ss.h, ss_b + kk + 1)
                                for half, pUx in ((0, pU), (1, pU2)):
                                    mm = tensor.matmul(
                                        pUx, cv_sb[:, kk * 128:(kk + 1) * 128],
                                        expb[kk % NEXP][:, half * 512:(half + 1) * 512],
                                        start=(kk == 0), stop=(kk == NJC - 1))
                                ts.inc(mm)
                        assert ts.n == ts_b + 2 * NJC
                        m_qklast[gp] = ts.n

                        NG = NJC // 8

                        def dtree_l3(vector, pr, g8):
                            c0 = pr * 1024
                            vs.wait(vector)
                            vs.inc(vector.tensor_tensor(t4[0], t4[0], t4[1],
                                                        ALU.add))
                            vs.wait(vector)
                            if g8 == 0:
                                vs.inc(vector.tensor_copy(dacc[:, c0:c0 + 1024],
                                                          t4[0]))
                            else:
                                vs.inc(vector.tensor_tensor(
                                    dacc[:, c0:c0 + 1024], dacc[:, c0:c0 + 1024],
                                    t4[0], ALU.add))

                        for g8 in range(NG):
                            @blk.vector
                            def _(vector, pr=pr, g8=g8, ss_b=ss_b, gp=gp):
                                kb = 8 * g8
                                if debug and g8 == 0 and gp == 0:
                                    vector.wait_ge(s_dbg.h, s_dbg.n)
                                for hv in range(2):
                                    k0 = kb + 4 * hv
                                    vector.wait_ge(ss.h, ss_b + k0 + 2)
                                    vs.inc(vector.tensor_tensor(
                                        ts2[0], expb[k0 % NEXP], expb[(k0 + 1) % NEXP],
                                        ALU.add))
                                    vector.wait_ge(ss.h, ss_b + k0 + 4)
                                    vs.inc(vector.tensor_tensor(
                                        ts2[1], expb[(k0 + 2) % NEXP],
                                        expb[(k0 + 3) % NEXP], ALU.add))
                                    vs.wait(vector)
                                    vs.inc(vector.tensor_tensor(t4[hv], ts2[0], ts2[1],
                                                                ALU.add))
                                if g8 < NG - 1:
                                    dtree_l3(vector, pr, g8)
                        assert vs.n == vs_b + NJC - 2

                        if pr < NDC - 1:
                            emit_qrot(gp + 1)
                            emit_qrope(gp + 1)

                        @blk.vector
                        def _(vector, pr=pr, gp=gp):
                            vector.wait_ge(ts.h, ts.n)
                            vs.inc(vector.tensor_copy(
                                u_raw[:, (2 * pr) * 512:(2 * pr + 1) * 512], pU))
                            vs.inc(vector.tensor_copy(
                                u_raw[:, (2 * pr + 1) * 512:(2 * pr + 2) * 512], pU2))
                        m_uev[gp] = vs.n

                        @blk.vector
                        def _(vector, pr=pr):
                            dtree_l3(vector, pr, NG - 1)
                        m_dacc_end[gp] = vs.n

                        @blk.sync
                        def _(sync, gp=gp):
                            if 2 <= gp + 2 < NIB * NDC:
                                issue_pair_loads(sync, gp + 2)

                    m_pairs_done = m_dacc_end[ib * NDC + NDC - 1]

                    if debug:
                        @blk.sync
                        def _(sync, ib=ib):
                            sync.wait_ge(vs.h, m_pairs_done)
                            s_dbg.inc(sync.dma_start(out=dbg["dacc"][ib], in_=dacc), 16)
                            sync.wait_ge(s_dbg.h, s_dbg.n)

                    dt_b, ds_b, dv_b = ts.n, ss.n, vs.n

                    pDS4 = [pU[0:1, :], pU2[0:1, :], pU[32:33, :], pU2[32:33, :]]

                    @blk.scalar
                    def _(scalar, dt_b=dt_b):
                        for h in range(H):
                            pDS = pDS4[h % 4]
                            dsc = h + 1 if h < 2 else 2 * h
                            scalar.wait_ge(ts.h, dt_b + dsc)
                            lg = dlgr[0:1, (h % 2) * 512:(h % 2) * 512 + 512]
                            dv_ = dinvr[0:1, (h % 4) * 512:(h % 4) * 512 + 512]
                            ss.inc(scalar.activation(lg, pDS, AF.Ln))
                            ss.wait(scalar)
                            ss.inc(scalar.activation(dv_, lg, AF.Exp, scale=-1.0))
                    assert ss.n == ds_b + 2 * H

                    m_bc = {}

                    @blk.tensor
                    def _(tensor, ds_b=ds_b, dv_b=dv_b):
                        for h in range(H):
                            pDS = pDS4[h % 4]
                            if h < 2:
                                tensor.wait_ge(vs.h, m_pairs_done)
                            if h >= 4:
                                tensor.wait_ge(ss.h, ds_b + 2 * (h - 4) + 1)
                            ts.inc(tensor.matmul(pDS, oncf,
                                                 dacc[:, h * 512:(h + 1) * 512],
                                                 start=True, stop=True))
                            if h >= 1:
                                hb = h - 1
                                pBC = pQR if hb % 2 == 0 else pMS
                                tensor.wait_ge(ss.h, ds_b + 2 * hb + 2)
                                if hb >= 2:
                                    tensor.wait_ge(vs.h, dv_b + hb - 1)
                                ts.inc(tensor.matmul(pBC, onrb,
                                                     dinvr[0:1, (hb % 4) * 512:(hb % 4) * 512 + 512],
                                                     start=True, stop=True))
                                m_bc[hb] = ts.n
                        hb = H - 1
                        tensor.wait_ge(ss.h, ds_b + 2 * hb + 2)
                        tensor.wait_ge(vs.h, dv_b + hb - 1)
                        ts.inc(tensor.matmul(pMS, onrb,
                                             dinvr[0:1, (hb % 4) * 512:(hb % 4) * 512 + 512],
                                             start=True, stop=True))
                        m_bc[hb] = ts.n

                    m_dsc = {}

                    @blk.vector
                    def _(vector):
                        for h in range(H):
                            pBC = pQR if h % 2 == 0 else pMS
                            vector.wait_ge(ts.h, m_bc[h])
                            c0 = h * 512
                            vs.inc(vector.tensor_tensor(u_raw[:, c0:c0 + 512],
                                                        u_raw[:, c0:c0 + 512], pBC, ALU.mult))
                            m_dsc[h] = vs.n
                    assert vs.n == dv_b + H
                    m_dscale_all = vs.n
                    m_dscale_prev[ib] = vs.n

                    def emit_attn(t_, ib=ib):
                        git = ib * 4 + t_
                        pAT = pSA if t_ % 2 == 0 else pSB

                        @blk.sync
                        def _(sync, git=git):
                            if git >= 2:
                                sync.wait_ge(vs.h, m_attnev.get(git - 2, 0))
                            s_x2.issue(sync, xp2[git % 2],
                                       x_in[git * 128:(git + 1) * 128, :], git)

                        @blk.tensor
                        def _(tensor, t_=t_, git=git, pAT=pAT):
                            if t_ == 0:
                                tensor.wait_ge(s_wvo.h, m_wvold)
                            tensor.wait_ge(vs.h, m_attnev_ps.get(git - 2, 0))
                            for h in range(H):
                                if t_ == 0:
                                    tensor.wait_ge(vs.h, m_dsc[h])
                                for nt in range(2):
                                    mm = tensor.matmul(
                                        pAT[:, nt * 512:(nt + 1) * 512],
                                        u_raw[:, h * 512 + t_ * 128: h * 512 + (t_ + 1) * 128],
                                        wvo_sb[:, h * 1024 + nt * 512: h * 1024 + (nt + 1) * 512],
                                        start=(h == 0), stop=(h == H - 1))
                            ts.inc(mm)
                        m_attn = ts.n

                        @blk.vector
                        def _(vector, git=git, m_attn=m_attn, pAT=pAT):
                            vector.wait_ge(ts.h, m_attn)
                            vs.inc(vector.tensor_tensor(x1s, pAT, cbias_bc, ALU.add))
                            s_x2.wait(vector, git)
                            s_x1w.wait(vector, git - 2)
                            vector.wait_ge(vs.h, m_norm2.get(git - 2, 0))
                            vs.wait(vector)
                            vs.inc(vector.tensor_tensor(x1t[git % 2], x1s, xp2[git % 2],
                                                        ALU.add))
                        m_attnev[git] = vs.n
                        m_attnev_ps[git] = m_attnev[git] - 1

                    def emit_ln2(t_, ib=ib):
                        git = ib * 4 + t_
                        m_x1 = m_attnev[git]

                        @blk.sync
                        def _(sync, git=git, m_x1=m_x1):
                            sync.wait_ge(vs.h, m_x1)
                            s_x1w.issue(sync, x1_scr[git * 128:(git + 1) * 128, :],
                                        x1t[git % 2], git)

                        @blk.vector
                        def _(vector, git=git, m_x1=m_x1):
                            vector.wait_ge(vs.h, m_x1)
                            vector.wait_ge(ss.h, m_rstd2[git - 1])
                            vector.bn_stats(stats2[:, 0:6], x1t[git % 2][:, 0:512])
                            vs.inc(vector.bn_stats(stats2[:, 6:12], x1t[git % 2][:, 512:1024]))
                            vs.wait(vector)
                            vs.inc(vector.bn_aggr(mv2, stats2))
                        m_agg2 = vs.n

                        @blk.scalar
                        def _(scalar, git=git, m_agg2=m_agg2):
                            scalar.wait_ge(vs.h, m_agg2)
                            scalar.wait_ge(vs.h, m_norm2[git - 1])
                            ss.inc(scalar.activation(lnv2, mv2[:, 1:2], AF.Ln,
                                                     bias=epst, scale=1.0))
                            ss.wait(scalar)
                            ss.inc(scalar.activation(rstd2, lnv2, AF.Exp, scale=-0.5))
                        m_rstd2[git] = ss.n

                        @blk.vector
                        def _(vector, git=git):
                            vector.wait_ge(ss.h, m_rstd2[git])
                            s_ln2w.wait(vector, git - 2)
                            vs.inc(vector.tensor_scalar(ln2tile[git % 2], x1t[git % 2],
                                                        mv2[:, 0:1], rstd2,
                                                        ALU.subtract, ALU.mult))
                        m_norm2[git] = vs.n

                        @blk.sync
                        def _(sync, git=git):
                            sync.wait_ge(vs.h, m_norm2[git])
                            s_ln2w.issue(sync, ln2t_scr[git * 128:(git + 1) * 128, :],
                                         ln2tile[git % 2], git)
                        m_p4t[git] = m_norm2[git]


                    for t_ in range(4):
                        emit_attn(t_)
                        if t_ >= 1:
                            emit_ln2(t_ - 1)
                    emit_ln2(3)

                    m_p4v[ib] = m_norm2[ib * 4 + 3]

                if debug:
                    @blk.sync
                    def _(sync):
                        s_x1w.wait_all(sync)
                        s_ln2w.wait_all(sync)
                        sync.wait_ge(vs.h, vs.n)
                        s_dbg.inc(sync.dma_start(out=dbg["x1"], in_=x1_scr), 16)
                        s_dbg.inc(sync.dma_start(out=dbg["ln2t"], in_=ln2t_scr), 16)
                        sync.wait_ge(s_dbg.h, s_dbg.n)

            m_ph2_v, m_ph2_s, m_ph2_t = vs.n, ss.n, ts.n

        with contextlib.ExitStack() as ph3:
            def sb3(name, shp, dt=F32):
                return ph3.enter_context(nc.sbuf_tensor(name, shp, dt))[:, :]
            wt3_sb = sb3("wt3_sb", [128, NHT * D], BF16)
            hid = sb3("hid", [128, NHT * FIB], BF16)
            ln2t_ib = [sb3(f"ln2t_ib{i}", [128, NDC * FIB], BF16) for i in range(2)]
            acc = sb3("acc", [128, FIT * D])
            w12b = [sb3(f"w12b{i}", [128, 4096], BF16) for i in range(2)]
            sil = sb3("sil", [128, FIB], BF16)
            x1rb = [sb3(f"x1rb{i}", [128, D]) for i in range(2)]
            gt = sb3("gt", [128, NE])
            nmax1 = sb3("nmax1", [128, 1])
            gmax2 = sb3("gmax2", [128, 1])
            gdf = sb3("gdf", [128, NE])
            gmask = sb3("gmask", [128, NE])
            gexp = sb3("gexp", [128, NE])
            gme = sb3("gme", [128, NE])
            gsum = sb3("gsum", [128, 1])
            glog = sb3("glog", [128, 1])
            gden = sb3("gden", [128, 1])
            ytb = [sb3(f"ytb{i}", [128, D]) for i in range(2)]
            pFA = ph3.enter_context(nc.psum_tensor("pFA", [128, 2 * FIB], F32))[:, :]
            pFB = ph3.enter_context(nc.psum_tensor("pFB", [128, 2 * FIB], F32))[:, :]
            NN = FIB // 512

            @blk.tensor
            def _(tensor):
                tensor.wait_ge(vs.h, m_ph2_v)
                tensor.wait_ge(ss.h, m_ph2_s)

            @blk.vector
            def _(vector):
                vector.wait_ge(ts.h, m_ph2_t)
                vector.wait_ge(ss.h, m_ph2_s)

            @blk.scalar
            def _(scalar):
                scalar.wait_ge(ts.h, m_ph2_t)
                scalar.wait_ge(vs.h, m_ph2_v)

            m_hmm = {}
            m_silu = {}
            m_hid = {}
            m_eomm = {}
            m_accv = {}
            m_yv = {}
            m_w12_user = {}
            m_ln2ld = {}

            def w12_aps(e, wb):
                return (wt1_bf[e, 2 * wb:2 * wb + 2].rearrange("h c p f -> p h c f"),
                        wt2_bf[e, 2 * wb:2 * wb + 2].rearrange("h c p f -> p h c f"))

            def wait_w12(eng, gw):
                eng.wait_ge(s_w12.s[gw % 2].h, 32 * (gw // 2 + 1))

            def issue_ln2_load(sync, fb):
                for dc in range(NDC):
                    s_ln2r.inc(sync.dma_start_transpose(
                        ln2t_ib[fb % 2][:, dc * FIB:(dc + 1) * FIB],
                        ln2t_scr[fb * FIB:(fb + 1) * FIB, dc * 128:(dc + 1) * 128]), 16)
                m_ln2ld[fb] = s_ln2r.n

            for fb in range(NFB):
                @blk.sync
                def _(sync, fb=fb):
                    if fb == 0:
                        sync.wait_ge(ts.h, m_ph2_t)
                        sync.wait_ge(vs.h, m_ph2_v)
                        sync.wait_ge(ss.h, m_ph2_s)
                        s_ln2w.wait_all(sync)
                        issue_ln2_load(sync, 0)
                        if NFB > 1:
                            issue_ln2_load(sync, 1)
                    elif fb + 1 < NFB:
                        sync.wait_ge(ts.h, m_hmm[(fb - 1, NE - 1, NHT - 1)])
                        issue_ln2_load(sync, fb + 1)

                m_git = {}
                for it_ in range(FIT):
                    git = fb * FIT + it_

                    @blk.tensor
                    def _(tensor, fb=fb, it_=it_, git=git):
                        if it_ == 0:
                            tensor.wait_ge(s_ln2r.h, m_ln2ld[fb])
                            if fb > 0:
                                tensor.wait_ge(vs.h, m_accv.get(
                                    ((fb - 1) * NE + NE - 1, FIT - 2), 0))
                        if it_ > 0:
                            tensor.wait_ge(vs.h, m_git[git - 1] - 10)
                        for dc in range(NDC):
                            mm = tensor.matmul(
                                pFA[:, 0:NE],
                                ln2t_ib[fb % 2][:, dc * FIB + it_ * 128: dc * FIB + it_ * 128 + 128],
                                wgT_res[:, dc * NE:(dc + 1) * NE],
                                start=(dc == 0), stop=(dc == NDC - 1))
                        ts.inc(mm)
                    m_lg = ts.n

                    @blk.vector
                    def _(vector, m_lg=m_lg):
                        vector.wait_ge(ts.h, m_lg)
                        vs.inc(vector.tensor_copy(gt, pFA[:, 0:NE]))
                        vs.wait(vector)
                        vs.inc(vector.reduce_max(nmax1, gt, axis=AX.X, negate=True))
                        vs.wait(vector)
                        vs.inc(vector.tensor_scalar(gdf, gt, nmax1, None, ALU.add))
                        vs.wait(vector)
                        vs.inc(vector.tensor_scalar(gmask, gdf, 0.0, None, ALU.is_ge))
                        vs.wait(vector)
                        vs.inc(vector.scalar_tensor_tensor(gdf, gmask, -3.0e38, gt,
                                                           ALU.mult, ALU.add))
                        vs.wait(vector)
                        vs.inc(vector.reduce_max(gmax2, gdf, axis=AX.X))
                        vs.wait(vector)
                        vs.inc(vector.tensor_scalar(gmask, gt, gmax2, None, ALU.is_ge))
                    m_gv = vs.n

                    @blk.scalar
                    def _(scalar, m_gv=m_gv):
                        scalar.wait_ge(vs.h, m_gv)
                        ss.inc(scalar.activation(gexp, gt, AF.Exp, bias=nmax1, scale=1.0))
                    m_gexp = ss.n

                    @blk.vector
                    def _(vector, git=git, m_gexp=m_gexp):
                        vector.wait_ge(ss.h, m_gexp)
                        vs.inc(vector.tensor_tensor(gme, gexp, gmask, ALU.mult))
                        vs.wait(vector)
                        vs.inc(vector.reduce_sum(gsum, gme, axis=AX.X))
                        vs.wait(vector)
                        vs.inc(vector.reciprocal(gden, gsum))
                        vs.wait(vector)
                        vs.inc(vector.tensor_scalar(gbuf[:, git * NE:(git + 1) * NE],
                                                    gme, gden, None, ALU.mult))
                    m_git[git] = vs.n

                for e in range(NE):
                    ge = fb * NE + e

                    @blk.sync
                    def _(sync, e=e, ge=ge):
                        if ge == 0:
                            sync.wait_ge(g_w.h, m_wcast)
                        else:
                            sync.wait_ge(ts.h, m_eomm[(ge - 1, FIT - 1)])
                        s_w3.inc(sync.dma_start(
                            out=wt3_sb[:, :].rearrange("p (c f) -> p c f", c=NHT),
                            in_=wt3_bf[e].rearrange("c p f -> p c f")), 16)
                    m_w3ld = s_w3.n

                    hts_b, hss_b, hvs_b = ts.n, ss.n, vs.n
                    for ht in range(NHT):
                        m_hmm[(fb, e, ht)] = hts_b + ht + 1
                        m_silu[(ge, ht)] = hss_b + ht + 1
                        m_hid[(ge, ht)] = hvs_b + ht + 1
                    for wb in range(NW):
                        m_w12_user[ge * NW + wb] = hts_b + 2 * wb + 2

                    @blk.gpsimd
                    def _(gpsimd, e=e, ge=ge):
                        if ge == 0:
                            gpsimd.wait_ge(g_w.h, m_wcast)
                            gpsimd.wait_ge(ts.h, m_ph2_t)
                            gpsimd.wait_ge(vs.h, m_ph2_v)
                            gpsimd.wait_ge(ss.h, m_ph2_s)
                            s_ln2w.wait_all(gpsimd)
                            s_x1w.wait_all(gpsimd)
                            if debug:
                                gpsimd.wait_ge(s_dbg.h, s_dbg.n)
                        for wb in range(NW):
                            gw = ge * NW + wb
                            gpsimd.wait_ge(ts.h, m_w12_user.get(gw - 2, 0))
                            a1, a2 = w12_aps(e, wb)
                            sm = s_w12.s[gw % 2]
                            sm.inc(gpsimd.dma_start(
                                out=w12b[gw % 2][:, 0:2048]
                                .rearrange("p (h c f) -> p h c f", h=2, c=NDC), in_=a1), 16)
                            sm.inc(gpsimd.dma_start(
                                out=w12b[gw % 2][:, 2048:4096]
                                .rearrange("p (h c f) -> p h c f", h=2, c=NDC), in_=a2), 16)

                    @blk.scalar
                    def _(scalar, fb=fb, e=e, ge=ge, hts_b=hts_b, hvs_b=hvs_b):
                        for ht in range(NHT):
                            pF = pFA if (ht % 2 == 0) else pFB
                            scalar.wait_ge(ts.h, hts_b + ht + 1)
                            if ht > 0:
                                scalar.wait_ge(vs.h, hvs_b + ht)
                            ss.inc(scalar.activation(sil, pF[:, 0:FIB], AF.Silu))
                    assert ss.n == hss_b + NHT

                    for ht in range(NHT):
                        wb, hi = ht // 2, ht % 2
                        gw = ge * NW + wb
                        pF = pFA if (ht % 2 == 0) else pFB

                        @blk.tensor
                        def _(tensor, fb=fb, e=e, ge=ge, ht=ht, hi=hi, gw=gw, pF=pF,
                              hvs_b=hvs_b):
                            if ht == 0:
                                tensor.wait_ge(s_ln2r.h, m_ln2ld[fb])
                                tensor.wait_ge(vs.h, m_accv.get((ge - 1, FIT - 1), 0))
                                if e == 0:
                                    tensor.wait_ge(vs.h, m_git[fb * FIT + FIT - 1])
                            wait_w12(tensor, gw)
                            if ht >= 2:
                                tensor.wait_ge(vs.h, hvs_b + ht - 1)
                            base1 = hi * 1024
                            base2 = 2048 + hi * 1024
                            for dc in range(NDC):
                                for nn in range(NN):
                                    tensor.matmul(
                                        pF[:, nn * 512:(nn + 1) * 512],
                                        w12b[gw % 2][:, base1 + dc * 128: base1 + (dc + 1) * 128],
                                        ln2t_ib[fb % 2][:, dc * FIB + nn * 512: dc * FIB + (nn + 1) * 512],
                                        start=(dc == 0), stop=(dc == NDC - 1))
                            for dc in range(NDC):
                                for nn in range(NN):
                                    mm = tensor.matmul(
                                        pF[:, FIB + nn * 512: FIB + (nn + 1) * 512],
                                        w12b[gw % 2][:, base2 + dc * 128: base2 + (dc + 1) * 128],
                                        ln2t_ib[fb % 2][:, dc * FIB + nn * 512: dc * FIB + (nn + 1) * 512],
                                        start=(dc == 0), stop=(dc == NDC - 1))
                            ts.inc(mm)
                    assert ts.n == hts_b + NHT

                    @blk.vector
                    def _(vector, ge=ge, hss_b=hss_b):
                        for ht in range(NHT):
                            pF = pFA if (ht % 2 == 0) else pFB
                            vector.wait_ge(ss.h, hss_b + ht + 1)
                            vs.inc(vector.tensor_tensor(hid[:, ht * FIB:(ht + 1) * FIB],
                                                        sil, pF[:, FIB:2 * FIB], ALU.mult))
                    assert vs.n == hvs_b + NHT


                    for i128 in range(FIT):
                        pE = pFA[:, 0:1024] if (i128 % 2 == 0) else pFB[:, 0:1024]
                        git = fb * FIT + i128

                        @blk.tensor
                        def _(tensor, ge=ge, i128=i128, pE=pE):
                            if i128 == 0:
                                tensor.wait_ge(vs.h, m_hid[(ge, NHT - 1)])
                                tensor.wait_ge(s_w3.h, m_w3ld)
                            tensor.wait_ge(vs.h, m_accv.get((ge, i128 - 2), 0))
                            for hc in range(NHT):
                                for nn in range(2):
                                    mm = tensor.matmul(
                                        pE[:, nn * 512:(nn + 1) * 512],
                                        hid[:, hc * FIB + i128 * 128: hc * FIB + (i128 + 1) * 128],
                                        wt3_sb[:, hc * D + nn * 512: hc * D + (nn + 1) * 512],
                                        start=(hc == 0), stop=(hc == NHT - 1))
                            ts.inc(mm)
                        m_eomm[(ge, i128)] = ts.n

                        @blk.vector
                        def _(vector, fb=fb, e=e, ge=ge, i128=i128, pE=pE, git=git):
                            vector.wait_ge(ts.h, m_eomm[(ge, i128)])
                            gcol = gbuf[:, git * NE + e: git * NE + e + 1]
                            a0 = i128 * D
                            if e == 0:
                                vector.wait_ge(vs.h, m_yv.get(git - FIT, 0))
                                vs.inc(vector.tensor_scalar(acc[:, a0:a0 + D], pE,
                                                            gcol, None, ALU.mult))
                            else:
                                vs.inc(vector.scalar_tensor_tensor(acc[:, a0:a0 + D], pE,
                                                                   gcol, acc[:, a0:a0 + D],
                                                                   ALU.mult, ALU.add))
                        m_accv[(ge, i128)] = vs.n

                        if e == NE - 1:
                            @blk.sync
                            def _(sync, git=git):
                                if git == 0:
                                    s_x1w.wait_all(sync)
                                if git >= 2:
                                    sync.wait_ge(vs.h, m_yv.get(git - 2, 0))
                                s_x1r.issue(sync, x1rb[git % 2],
                                            x1_scr[git * 128:(git + 1) * 128, :], git)

                            @blk.vector
                            def _(vector, ge=ge, git=git, i128=i128):
                                s_x1r.wait(vector, git)
                                vector.wait_ge(vs.h, m_accv[(ge, i128)])
                                s_y.wait(vector, git - 2)
                                a0 = i128 * D
                                vs.inc(vector.tensor_tensor(ytb[git % 2], acc[:, a0:a0 + D],
                                                            x1rb[git % 2], ALU.add))
                            m_yv[git] = vs.n

                            @blk.sync
                            def _(sync, git=git):
                                sync.wait_ge(vs.h, m_yv[git])
                                s_y.issue(sync, y_out[git * 128:(git + 1) * 128, :],
                                          ytb[git % 2], git)

            if debug:
                @blk.sync
                def _(sync):
                    sync.wait_ge(vs.h, vs.n)
                    s_dbg.inc(sync.dma_start(
                        out=dbg["g"].rearrange("t p e -> p t e"),
                        in_=gbuf[:, :].rearrange("p (t e) -> p t e", t=NIT)), 16)
                    sync.wait_ge(s_dbg.h, s_dbg.n)

            @blk.sync
            def _(sync):
                s_y.wait_all(sync)

    return nc


def _rope_tables(T):
    d = np.arange(0, DH, 2, dtype=np.float64)
    inv = 1.0 / ROPE_BASE ** (d / DH)
    t = np.arange(T, dtype=np.float64)
    fr = t[:, None] * inv[None, :]
    emb = np.concatenate([fr, fr], axis=-1)
    cos = np.cos(emb).T
    sin = np.sin(emb).T
    sign = np.ones((DH, 1))
    sign[0:32] = -1.0
    sins = sign * sin
    cosP = np.concatenate([cos, cos], axis=0)
    sinsP = np.concatenate([sins, sins], axis=0)
    return (np.ascontiguousarray(cosP.astype(ml_dtypes.bfloat16)),
            np.ascontiguousarray(sinsP.astype(ml_dtypes.bfloat16)))


def _host_consts(T):
    ident = np.eye(128, dtype=ml_dtypes.bfloat16)
    rrot = np.zeros((128, 128), dtype=ml_dtypes.bfloat16)
    for m in range(128):
        rrot[m ^ 32, m] = 1.0
    cosP, sinsP = _rope_tables(T)
    return dict(ident=ident, rrot=rrot, cosT=cosP, sinsT=sinsP,
                onc=np.ones((128, 1), dtype=ml_dtypes.bfloat16),
                oncf=np.ones((128, 1), dtype=np.float32),
                onr=np.ones((1, 128), dtype=np.float32))


def _host_weights(ins):
    f = np.ascontiguousarray
    w1 = np.asarray(ins["L0_ffn_W1"])
    w2 = np.asarray(ins["L0_ffn_W2"])
    w3 = np.asarray(ins["L0_ffn_W3"])
    pad = HPAD - HID

    def tile_w12(w):
        wt = w.transpose(0, 2, 1)
        wt = np.pad(wt, ((0, 0), (0, 0), (0, pad)))
        wt = wt.reshape(NE, NDC, 128, NHT, 128)
        return f(wt.transpose(0, 3, 1, 2, 4))

    def tile_w3(w):
        wt = w.transpose(0, 2, 1)
        wt = np.pad(wt, ((0, 0), (0, pad), (0, 0)))
        return f(wt.reshape(NE, NHT, 128, D))

    return dict(
        w_dk=f(np.asarray(ins["L0_attn_w_dk"])),
        w_dv=f(np.asarray(ins["L0_attn_w_dv"])),
        w_uk=f(np.asarray(ins["L0_attn_w_uk"])),
        w_uvT=f(np.asarray(ins["L0_attn_w_uv"]).T),
        w_o=f(np.asarray(ins["L0_attn_w_o"])),
        b_dk=f(np.asarray(ins["L0_attn_b_dk"]).reshape(C, 1)),
        b_dv=f(np.asarray(ins["L0_attn_b_dv"]).reshape(C, 1)),
        b_uk=f(np.asarray(ins["L0_attn_b_uk"]).reshape(NDC, 128).T),
        b_uv=f(np.asarray(ins["L0_attn_b_uv"]).reshape(NDC, 128).T),
        b_o=f(np.asarray(ins["L0_attn_b_o"]).reshape(1, D)),
        wgT=f(np.asarray(ins["L0_ffn_Wg"]).T),
        w1t=tile_w12(w1), w2t=tile_w12(w2), w3t=tile_w3(w3),
    )


def kernel(**ins):
    x = np.asarray(ins["L0_attn_x"])
    Bx, T, Dx = x.shape
    nc = build_core(T=T, debug=False)
    common = {}
    common.update(_host_weights(ins))
    common.update(_host_consts(T))
    in_maps = []
    for b in range(Bx):
        m = dict(common)
        m["x"] = np.ascontiguousarray(x[b])
        in_maps.append(m)
    res = run_bass_kernel_spmd(nc, in_maps, core_ids=list(range(Bx)))
    return np.stack([np.asarray(r["y"]) for r in res.results], axis=0)


# version 31 (final)
# speedup vs baseline: 1.1216x; 1.1216x over previous
import contextlib

import numpy as np
import ml_dtypes
import concourse.bass as bass
import concourse.mybir as mybir
from concourse.bass_utils import run_bass_kernel_spmd

F32 = mybir.dt.float32
BF16 = mybir.dt.bfloat16
AF = mybir.ActivationFunctionType
ALU = mybir.AluOpType
AX = mybir.AxisListType

B = 8
D = 1024
H = 16
DH = 64
C = 128
NE = 8
HID = 2730
HPAD = 2816
NHT = HPAD // 128
NDC = D // 128
ROPE_BASE = 10000.0
EPS = 1e-5


class Sem:
    def __init__(self, handle):
        self.h = handle
        self.n = 0

    def inc(self, instr, amount=1):
        instr.then_inc(self.h, amount)
        self.n += amount
        return self.n

    def wait(self, eng, n=None):
        eng.wait_ge(self.h, self.n if n is None else n)


class DStream:
    def __init__(self, mk, name):
        self.s = [mk(name + "a"), mk(name + "b")]

    def issue(self, eng, out, in_, idx):
        self.s[idx % 2].inc(eng.dma_start(out=out, in_=in_), 16)

    def wait(self, eng, idx):
        if idx < 0:
            return
        eng.wait_ge(self.s[idx % 2].h, 16 * (idx // 2 + 1))

    def wait_all(self, eng):
        eng.wait_ge(self.s[0].h, self.s[0].n)
        eng.wait_ge(self.s[1].h, self.s[1].n)


def build_core(T=4096, debug=False):
    NIT = T // 128
    NIB = T // 512
    N512 = T // 512
    NK = T // 256
    FIB = 1024 if T >= 1024 else T
    NFB = T // FIB
    FIT = FIB // 128
    NW = NHT // 2

    nc = bass.Bass("TRN2", target_bir_lowering=False)

    x_in = nc.dram_tensor("x", [T, D], F32, kind="ExternalInput").ap()
    w_dk = nc.dram_tensor("w_dk", [D, C], F32, kind="ExternalInput").ap()
    w_dv = nc.dram_tensor("w_dv", [D, C], F32, kind="ExternalInput").ap()
    w_uk = nc.dram_tensor("w_uk", [C, D], F32, kind="ExternalInput").ap()
    w_uvT = nc.dram_tensor("w_uvT", [D, C], F32, kind="ExternalInput").ap()
    w_o = nc.dram_tensor("w_o", [D, D], F32, kind="ExternalInput").ap()
    b_dk = nc.dram_tensor("b_dk", [C, 1], F32, kind="ExternalInput").ap()
    b_dv = nc.dram_tensor("b_dv", [C, 1], F32, kind="ExternalInput").ap()
    b_uk = nc.dram_tensor("b_uk", [128, NDC], F32, kind="ExternalInput").ap()
    b_uv = nc.dram_tensor("b_uv", [128, NDC], F32, kind="ExternalInput").ap()
    b_o = nc.dram_tensor("b_o", [1, D], F32, kind="ExternalInput").ap()
    wgT = nc.dram_tensor("wgT", [D, NE], F32, kind="ExternalInput").ap()
    w1t = nc.dram_tensor("w1t", [NE, NHT, NDC, 128, 128], F32, kind="ExternalInput").ap()
    w2t = nc.dram_tensor("w2t", [NE, NHT, NDC, 128, 128], F32, kind="ExternalInput").ap()
    w3t = nc.dram_tensor("w3t", [NE, NHT, 128, D], F32, kind="ExternalInput").ap()
    ident_in = nc.dram_tensor("ident", [128, 128], BF16, kind="ExternalInput").ap()
    rrot_in = nc.dram_tensor("rrot", [128, 128], BF16, kind="ExternalInput").ap()
    cos_in = nc.dram_tensor("cosT", [128, T], BF16, kind="ExternalInput").ap()
    sins_in = nc.dram_tensor("sinsT", [128, T], BF16, kind="ExternalInput").ap()
    onc_in = nc.dram_tensor("onc", [128, 1], BF16, kind="ExternalInput").ap()
    oncf_in = nc.dram_tensor("oncf", [128, 1], F32, kind="ExternalInput").ap()
    onr_in = nc.dram_tensor("onr", [1, 128], F32, kind="ExternalInput").ap()

    y_out = nc.dram_tensor("y", [T, D], F32, kind="ExternalOutput").ap()

    wt1_bf = nc.dram_tensor("wt1_bf", [NE, NHT, NDC, 128, 128], BF16, kind="Internal").ap()
    wt2_bf = nc.dram_tensor("wt2_bf", [NE, NHT, NDC, 128, 128], BF16, kind="Internal").ap()
    wt3_bf = nc.dram_tensor("wt3_bf", [NE, NHT, 128, D], BF16, kind="Internal").ap()
    ln1t_scr = nc.dram_tensor("ln1t_scr", [NDC, 128, T], BF16, kind="Internal").ap()
    kt_scr = nc.dram_tensor("kt_scr", [NDC, 128, T], BF16, kind="Internal").ap()
    ln2t_scr = nc.dram_tensor("ln2t_scr", [T, D], BF16, kind="Internal").ap()
    x1_scr = nc.dram_tensor("x1_scr", [T, D], F32, kind="Internal").ap()
    wvo_scr = nc.dram_tensor("wvo_scr", [H, C, D], BF16, kind="Internal").ap()

    dbg = {}
    if debug:
        for nm, shp, dt in [("ln1t", [NDC, 128, T], BF16), ("ckt", [C, T], BF16),
                            ("cv", [128, T], BF16), ("kt", [NDC, 128, T], BF16),
                            ("x1", [T, D], F32), ("g", [NIT, 128, NE], F32),
                            ("ln2t", [T, D], BF16), ("negm", [128, NDC], F32),
                            ("dacc", [NIB, 128, H * 512], F32)]:
            dbg[nm] = nc.dram_tensor("dbg_" + nm, shp, dt, kind="ExternalOutput").ap()

    es = contextlib.ExitStack()
    with es:
        def sem(name):
            return Sem(es.enter_context(nc.semaphore(name)))

        ts, vs, ss = sem("ts"), sem("vs"), sem("ss")
        s_const = sem("s_const")
        g_attnw = sem("g_attnw")
        g_w = sem("g_w")
        s_wvow = sem("s_wvow")
        s_ln1w = DStream(sem, "ln1w")
        s_ktw = sem("s_ktw")
        s_x1 = DStream(sem, "x1ld")
        s_x1b = DStream(sem, "x1ldb")
        s_kt = DStream(sem, "ktld")
        s_ln1r = DStream(sem, "ln1r")
        s_wvo = sem("s_wvo")
        s_x2 = DStream(sem, "x2ld")
        s_x1w = DStream(sem, "x1w")
        s_ln2w = DStream(sem, "ln2w")
        s_w3 = sem("s_w3")
        s_w12 = DStream(sem, "w12")
        s_ln2r = sem("s_ln2r")
        s_x1r = DStream(sem, "x1r")
        s_y = DStream(sem, "yst")
        s_dbg = sem("s_dbg")

        blk = es.enter_context(nc.Block())

        def sbo(name, shp, dt=F32):
            return es.enter_context(nc.sbuf_tensor(name, shp, dt))[:, :]

        onc = sbo("onc_s", [128, 1], BF16)
        oncf = sbo("oncf_s", [128, 1], F32)
        onr = sbo("onr_s", [1, 128], F32)
        onrb = sbo("onrb_s", [1, 128], BF16)
        epst = sbo("epst", [128, 1], F32)
        identb = sbo("identb", [128, 128], BF16)
        gbuf = sbo("gbuf", [128, NIT * NE], F32)
        wgT_res = sbo("wgT_res", [128, NDC * NE], BF16)

        sca = contextlib.ExitStack()
        with sca:
            def sba(name, shp, dt=F32):
                return sca.enter_context(nc.sbuf_tensor(name, shp, dt))[:, :]

            rrotb = sba("rrotb", [128, 128], BF16)
            cosT = sba("cosT_s", [128, T], BF16)
            sinsT = sba("sinsT_s", [128, T], BF16)
            wdk_sb = sba("wdk_sb", [128, NDC * 128], BF16)
            wdv_sb = sba("wdv_sb", [128, NDC * 128], BF16)
            wuk_sb = sba("wuk_sb", [128, D], BF16)

            bdk_sb = sba("bdk_sb", [128, 1], F32)
            bdv_sb = sba("bdv_sb", [128, 1], F32)
            buk_sb = sba("buk_sb", [128, NDC], F32)
            cbias_bc = sba("cbias_bc", [128, D], F32)
            negM = sba("negM", [128, NDC], F32)
            cv_sb = sba("cv_sb", [128, T], BF16)
            ktp = [sba(f"ktp{i}", [128, T], BF16) for i in range(2)]

            @blk.sync
            def _(sync):
                for t_, d_ in [(identb, ident_in), (rrotb, rrot_in), (cosT, cos_in),
                               (sinsT, sins_in), (onc, onc_in), (oncf, oncf_in),
                               (onr, onr_in), (bdk_sb, b_dk), (bdv_sb, b_dv)]:
                    s_const.inc(sync.dma_start(out=t_, in_=d_), 16)
                s_const.inc(sync.dma_start(out=buk_sb, in_=b_uk), 16)
            m_const = s_const.n

            @blk.gpsimd
            def _(gpsimd):
                g_attnw.inc(gpsimd.dma_start(
                    out=wdk_sb[:, :].rearrange("p (c f) -> p c f", c=NDC),
                    in_=w_dk.rearrange("(c p) f -> p c f", p=128)), 16)
                g_attnw.inc(gpsimd.dma_start(
                    out=wdv_sb[:, :].rearrange("p (c f) -> p c f", c=NDC),
                    in_=w_dv.rearrange("(c p) f -> p c f", p=128)), 16)
                g_attnw.inc(gpsimd.dma_start(out=wuk_sb, in_=w_uk), 16)
                g_attnw.inc(gpsimd.dma_start(
                    out=wgT_res[:, :].rearrange("p (c f) -> p c f", c=NDC),
                    in_=wgT.rearrange("(c p) f -> p c f", p=128)), 16)
            m_attnw1 = g_attnw.n

            with contextlib.ExitStack() as ph0:
                def sb0(name, shp, dt=F32):
                    return ph0.enter_context(nc.sbuf_tensor(name, shp, dt))[:, :]
                wuvT_sb = sb0("wuvT_sb", [128, NDC * 128], BF16)
                wo_sb = sb0("wo_sb", [128, NDC * D], BF16)
                buv_f = sb0("buv_f", [128, NDC], F32)
                buv_bf = sb0("buv_bf", [128, NDC], BF16)
                wvo_st = sb0("wvo_st", [128, D], BF16)
                crow = sb0("crow", [1, D], F32)
                crow2 = sb0("crow2", [1, D], F32)
                bo_sb = sb0("bo_sb", [1, D], F32)
                pW = ph0.enter_context(nc.psum_tensor("pW", [128, 1024], F32))[:, :]
                pCB = ph0.enter_context(nc.psum_tensor("pCB", [1, 1024], F32))[:, :]

                @blk.sync
                def _(sync):
                    s_const.inc(sync.dma_start(out=buv_f, in_=b_uv), 16)
                    s_const.inc(sync.dma_start(out=bo_sb, in_=b_o), 16)
                m_const2 = s_const.n

                @blk.gpsimd
                def _(gpsimd):
                    g_attnw.inc(gpsimd.dma_start(
                        out=wuvT_sb[:, :].rearrange("p (c f) -> p c f", c=NDC),
                        in_=w_uvT.rearrange("(c p) f -> p c f", p=128)), 16)
                    g_attnw.inc(gpsimd.dma_start(
                        out=wo_sb[:, :].rearrange("p (c f) -> p c f", c=NDC),
                        in_=w_o.rearrange("(c p) f -> p c f", p=128)), 16)
                m_attnw = g_attnw.n

                @blk.vector
                def _(vector):
                    vector.wait_ge(s_const.h, m_const2)
                    vs.inc(vector.memset(epst, EPS))
                    vs.inc(vector.memset(onrb, 1.0))
                    vs.inc(vector.tensor_copy(buv_bf, buv_f))
                m_buv = vs.n

                wvo_ev = {-1: 0}
                for h in range(H):
                    dc, r0 = h // 2, (h % 2) * 64

                    @blk.tensor
                    def _(tensor, h=h, dc=dc, r0=r0):
                        if h == 0:
                            tensor.wait_ge(g_attnw.h, m_attnw)
                        tensor.wait_ge(vs.h, wvo_ev[h - 1])
                        for nt in range(2):
                            mm = tensor.matmul(
                                pW[:, nt * 512:(nt + 1) * 512],
                                wuvT_sb[r0:r0 + 64, dc * 128:(dc + 1) * 128],
                                wo_sb[r0:r0 + 64, dc * D + nt * 512: dc * D + (nt + 1) * 512],
                                start=True, stop=True)
                        ts.inc(mm)
                    m_wmm = ts.n

                    @blk.vector
                    def _(vector, h=h, m_wmm=m_wmm):
                        vector.wait_ge(ts.h, m_wmm)
                        if h >= 1:
                            vector.wait_ge(s_wvow.h, 16 * h)
                        vs.inc(vector.tensor_copy(wvo_st, pW))
                    wvo_ev[h] = vs.n

                    @blk.sync
                    def _(sync, h=h):
                        sync.wait_ge(vs.h, wvo_ev[h])
                        s_wvow.inc(sync.dma_start(out=wvo_scr[h], in_=wvo_st), 16)

                @blk.tensor
                def _(tensor):
                    tensor.wait_ge(vs.h, m_buv)
                    for nt in range(2):
                        for dc in range(NDC):
                            mm = tensor.matmul(
                                pCB[0:1, nt * 512:(nt + 1) * 512],
                                buv_bf[:, dc:dc + 1],
                                wo_sb[:, dc * D + nt * 512: dc * D + (nt + 1) * 512],
                                start=(dc == 0), stop=(dc == NDC - 1))
                    ts.inc(mm)
                m_cb = ts.n

                @blk.scalar
                def _(scalar):
                    scalar.wait_ge(ts.h, m_cb)
                    ss.inc(scalar.copy(crow, pCB))
                m_cr = ss.n

                @blk.vector
                def _(vector):
                    vector.wait_ge(ss.h, m_cr)
                    vs.inc(vector.tensor_tensor(crow2, crow, bo_sb, ALU.add))
                m_cr2 = vs.n

                @blk.tensor
                def _(tensor):
                    tensor.wait_ge(vs.h, m_cr2)
                    tensor.wait_ge(vs.h, wvo_ev[H - 1])
                    ts.inc(tensor.matmul(pW[:, 0:512], onr, crow2[:, 0:512],
                                         start=True, stop=True))
                    ts.inc(tensor.matmul(pW[:, 512:1024], onr, crow2[:, 512:1024],
                                         start=True, stop=True))
                m_cbc = ts.n

                @blk.scalar
                def _(scalar):
                    scalar.wait_ge(ts.h, m_cbc)
                    scalar.wait_ge(s_wvow.h, 16 * H)
                    ss.inc(scalar.copy(cbias_bc, pW))

            @blk.gpsimd
            def _(gpsimd):
                gpsimd.wait_ge(s_ktw.h, 16 * NDC)
                NLOAD = NIB * NDC - 2
                ci = 0
                for e in range(NE):
                    for dst, src in ((wt1_bf, w1t), (wt2_bf, w2t), (wt3_bf, w3t)):
                        for h0, h1 in ((0, NHT // 2), (NHT // 2, NHT)):
                            idx = min(ci, NLOAD - 1)
                            gpsimd.wait_ge(s_kt.s[idx % 2].h, 16 * (idx // 2 + 1))
                            g_w.inc(gpsimd.dma_start(out=dst[e, h0:h1],
                                                     in_=src[e, h0:h1]), 16)
                            ci += 1
            m_wcast = g_w.n

            m_ph0_t, m_ph0_v, m_ph0_s = ts.n, vs.n, ss.n
            m_const_all = s_const.n

            with contextlib.ExitStack() as ph1:
                def sb1(name, shp, dt=F32):
                    return ph1.enter_context(nc.sbuf_tensor(name, shp, dt))[:, :]
                xt = [sb1(f"xt{i}", [128, D]) for i in range(4)]
                stats = [sb1(f"stats{i}", [128, 12]) for i in range(2)]
                mv = [sb1(f"mv{i}", [128, 2]) for i in range(2)]
                lnv = [sb1(f"lnv{i}", [128, 1]) for i in range(2)]
                rstd = [sb1(f"rstd{i}", [128, 1]) for i in range(2)]
                lnt = [sb1(f"lnt{i}", [128, D], BF16) for i in range(2)]
                tstage = [sb1(f"tstage{i}", [128, NDC * 128], BF16) for i in range(2)]
                ckT = sb1("ckT", [128, T], BF16)
                cvT_st = [sb1(f"cvT_st{i}", [128, 128], BF16) for i in range(2)]
                ktmp_bf = sb1("ktmp_bf", [128, 512], BF16)
                ropeA = sb1("ropeA", [128, 512], BF16)
                ropeB = sb1("ropeB", [128, 512], BF16)
                sqf = sb1("sqf", [128, T], BF16)
                mxt = sb1("mxt", [1, N512])
                mxs = sb1("mxs", [1, 1])
                mlog = sb1("mlog", [1, 1])
                mexp = sb1("mexp", [1, 1])
                mneg = sb1("mneg", [1, 1])
                pT = ph1.enter_context(nc.psum_tensor("pT", [128, 1024], F32))[:, :]
                pSm = ph1.enter_context(nc.psum_tensor("pSm", [128, 512], F32))[:, :]
                pCk = pSm[:, 0:128]
                pCv = pSm[:, 128:256]
                pBc = pSm[:, 256:257]
                pCvT = ph1.enter_context(nc.psum_tensor("pCvT", [128, 128], F32))[:, :]
                pK = ph1.enter_context(nc.psum_tensor("pK", [128, 1024], F32))[:, :]
                pKR = ph1.enter_context(nc.psum_tensor("pKR", [128, 512], F32))[:, :]
                pMx = ph1.enter_context(nc.psum_tensor("pMx", [1, 512], F32))[:, :]

                m_agg = {-1: 0}
                m_rstdd = {-2: 0, -1: 0}
                m_norm = {-4: 0, -3: 0, -2: 0, -1: 0}
                m_tr = {-2: 0, -1: 0}
                m_stage = {-2: 0, -1: 0}
                m_cmm = {-2: 0, -1: 0}
                m_cev = {-2: 0, -1: 0}
                m_cvtr = {-2: 0, -1: 0}
                m_cvev = {-2: 0, -1: 0}

                def emit_ln_a(it):
                    p = it % 2
                    p4 = it % 4
                    q = it % 2

                    @blk.sync
                    def _(sync, it=it, p4=p4):
                        if it < 4:
                            sync.wait_ge(ts.h, m_ph0_t)
                            sync.wait_ge(vs.h, m_ph0_v)
                            sync.wait_ge(ss.h, m_ph0_s)
                        if it >= 4:
                            sync.wait_ge(vs.h, m_norm[it - 4])
                        s_x1.issue(sync, xt[p4][:, 0:512],
                                   x_in[it * 128:(it + 1) * 128, 0:512], it)

                    @blk.scalar
                    def _(scalar, it=it, p4=p4):
                        if it < 4:
                            scalar.wait_ge(ts.h, m_ph0_t)
                            scalar.wait_ge(vs.h, m_ph0_v)
                        if it >= 4:
                            scalar.wait_ge(vs.h, m_norm[it - 4])
                        s_x1b.issue(scalar, xt[p4][:, 512:1024],
                                    x_in[it * 128:(it + 1) * 128, 512:1024], it)

                    @blk.vector
                    def _(vector, it=it, p4=p4, q=q):
                        s_x1.wait(vector, it)
                        s_x1b.wait(vector, it)
                        vector.wait_ge(vs.h, m_norm[it - 2])
                        vector.wait_ge(ss.h, m_rstdd[it - 2])
                        vector.bn_stats(stats[q][:, 0:6], xt[p4][:, 0:512])
                        vs.inc(vector.bn_stats(stats[q][:, 6:12], xt[p4][:, 512:1024]))
                        vs.wait(vector)
                        vs.inc(vector.bn_aggr(mv[q], stats[q]))
                    m_agg[it] = vs.n

                    @blk.scalar
                    def _(scalar, it=it, q=q):
                        scalar.wait_ge(vs.h, m_agg[it])
                        scalar.wait_ge(vs.h, m_norm[it - 2])
                        ss.inc(scalar.activation(lnv[q], mv[q][:, 1:2], AF.Ln,
                                                 bias=epst, scale=1.0))
                        ss.wait(scalar)
                        ss.inc(scalar.activation(rstd[q], lnv[q], AF.Exp, scale=-0.5))
                    m_rstdd[it] = ss.n

                    @blk.vector
                    def _(vector, it=it, p=p, p4=p4, q=q):
                        vector.wait_ge(ss.h, m_rstdd[it])
                        vector.wait_ge(ts.h, m_tr.get(it - 2, 0))
                        vs.inc(vector.tensor_scalar(lnt[p], xt[p4], mv[q][:, 0:1],
                                                    rstd[q],
                                                    ALU.subtract, ALU.mult))
                    m_norm[it] = vs.n

                pT_x = [pT, pK]
                pCk_x = [pCk, pKR[:, 0:128]]
                pCv_x = [pCv, pKR[:, 128:256]]
                pCvT_x = [pCvT, pKR[:, 256:384]]

                def emit_ln_b(it):
                    p = it % 2
                    q = it % 2

                    @blk.tensor
                    def _(tensor, it=it, p=p, q=q):
                        tensor.wait_ge(vs.h, m_norm[it])
                        if it == 0:
                            tensor.wait_ge(s_const.h, m_const_all)
                        tensor.wait_ge(vs.h, m_stage[it - 2])
                        for dc in range(NDC):
                            mm = tensor.matmul(pT_x[q][:, dc * 128:(dc + 1) * 128],
                                               lnt[p][:, dc * 128:(dc + 1) * 128], identb,
                                               start=True, stop=True)
                        ts.inc(mm)
                    m_tr[it] = ts.n

                    @blk.vector
                    def _(vector, it=it, p=p, q=q):
                        vector.wait_ge(ts.h, m_tr[it])
                        vector.wait_ge(ts.h, m_cmm[it - 2])
                        s_ln1w.wait(vector, it - 2)
                        vs.inc(vector.tensor_copy(tstage[p], pT_x[q]))
                    m_stage[it] = vs.n

                    @blk.tensor
                    def _(tensor, it=it, p=p, q=q):
                        tensor.wait_ge(vs.h, m_stage[it])
                        if it == 0:
                            tensor.wait_ge(g_attnw.h, m_attnw)
                        tensor.wait_ge(vs.h, m_cev[it - 2])
                        for dc in range(NDC):
                            tensor.matmul(pCk_x[q], wdk_sb[:, dc * 128:(dc + 1) * 128],
                                          tstage[p][:, dc * 128:(dc + 1) * 128],
                                          start=(dc == 0), stop=(dc == NDC - 1))
                        for dc in range(NDC):
                            mm = tensor.matmul(pCv_x[q], wdv_sb[:, dc * 128:(dc + 1) * 128],
                                               tstage[p][:, dc * 128:(dc + 1) * 128],
                                               start=(dc == 0), stop=(dc == NDC - 1))
                        ts.inc(mm)
                    m_cmm[it] = ts.n

                    @blk.vector
                    def _(vector, it=it, q=q):
                        vector.wait_ge(ts.h, m_cmm[it])
                        vector.wait_ge(ts.h, m_cvtr[it - 2])
                        vs.inc(vector.tensor_scalar(ckT[:, it * 128:(it + 1) * 128],
                                                    pCk_x[q], bdk_sb, None, ALU.add))
                        vs.inc(vector.tensor_scalar(cvT_st[q], pCv_x[q],
                                                    bdv_sb, None, ALU.add))
                    m_cev[it] = vs.n

                    @blk.tensor
                    def _(tensor, it=it, q=q):
                        tensor.wait_ge(vs.h, m_cev[it])
                        tensor.wait_ge(vs.h, m_cvev[it - 2])
                        ts.inc(tensor.matmul(pCvT_x[q], cvT_st[q], identb,
                                             start=True, stop=True))
                    m_cvtr[it] = ts.n

                    @blk.vector
                    def _(vector, it=it, q=q):
                        vector.wait_ge(ts.h, m_cvtr[it])
                        vs.inc(vector.tensor_copy(cv_sb[:, it * 128:(it + 1) * 128],
                                                  pCvT_x[q]))
                    m_cvev[it] = vs.n

                    @blk.sync
                    def _(sync, it=it, p=p):
                        sync.wait_ge(vs.h, m_stage[it])
                        s_ln1w.issue(
                            sync,
                            ln1t_scr.rearrange("c p t -> p c t")[:, :, it * 128:(it + 1) * 128],
                            tstage[p][:, :].rearrange("p (c f) -> p c f", c=NDC), it)


                for it in range(NIT):
                    emit_ln_a(it)
                    if it >= 1:
                        emit_ln_b(it - 1)
                emit_ln_b(NIT - 1)

                m_ck_all = m_cev[NIT - 1]
                m_kbf = {-2: 0, -1: 0}
                m_krope = {-1: 0}
                m_krotmm = {-1: 0}
                m_pmx_free = 0
                m_pbc_free = 0
                m_sq_readers = 0

                for jp in range(NDC):
                    pr = (jp + 2) % NDC
                    for it2 in range(N512):
                        i0 = it2 * 512
                        gk = jp * N512 + it2
                        q = it2 % 2

                        @blk.tensor
                        def _(tensor, pr=pr, i0=i0, q=q, gk=gk):
                            if gk == 0:
                                tensor.wait_ge(vs.h, m_ck_all)
                            tensor.wait_ge(vs.h, m_kbf[gk - 2])
                            ts.inc(tensor.matmul(pK[:, q * 512:(q + 1) * 512],
                                                 wuk_sb[:, pr * 128:(pr + 1) * 128],
                                                 ckT[:, i0:i0 + 512],
                                                 start=True, stop=True))
                        m_kmm = ts.n

                        @blk.vector
                        def _(vector, pr=pr, q=q, gk=gk, m_kmm=m_kmm):
                            vector.wait_ge(ts.h, m_kmm)
                            vector.wait_ge(ts.h, m_krotmm[gk - 1])
                            vs.inc(vector.tensor_scalar(ktmp_bf, pK[:, q * 512:(q + 1) * 512],
                                                        buk_sb[:, pr:pr + 1], None, ALU.add))
                        m_kbf[gk] = vs.n

                        @blk.tensor
                        def _(tensor, gk=gk):
                            tensor.wait_ge(vs.h, m_kbf[gk])
                            tensor.wait_ge(vs.h, m_krope[gk - 1])
                            ts.inc(tensor.matmul(pKR, rrotb, ktmp_bf, start=True, stop=True))
                        m_krotmm[gk] = ts.n

                        @blk.vector
                        def _(vector, i0=i0, gk=gk, pr=pr, it2=it2, jp=jp):
                            vector.wait_ge(ts.h, m_krotmm[gk])
                            if it2 == 0 and jp >= 2:
                                vector.wait_ge(s_ktw.h, 16 * (jp - 1))
                            vs.inc(vector.tensor_tensor(ropeA, ktmp_bf,
                                                        cosT[:, i0:i0 + 512], ALU.mult))
                            vs.inc(vector.tensor_tensor(ropeB, pKR,
                                                        sinsT[:, i0:i0 + 512], ALU.mult))
                            vs.wait(vector)
                            vs.inc(vector.tensor_tensor(ktp[pr % 2][:, i0:i0 + 512],
                                                        ropeA, ropeB, ALU.add))
                        m_krope[gk] = vs.n

                    @blk.vector
                    def _(vector, pr=pr, jp=jp):
                        vector.wait_ge(vs.h, m_krope[jp * N512 + N512 - 1])
                        vector.wait_ge(ts.h, m_sq_readers)
                        vs.inc(vector.tensor_tensor(sqf, ktp[pr % 2],
                                                    ktp[pr % 2], ALU.mult))
                    m_sq = vs.n

                    for it2 in range(N512):
                        @blk.tensor
                        def _(tensor, it2=it2, m_sq=m_sq):
                            tensor.wait_ge(vs.h, m_sq)
                            tensor.wait_ge(vs.h, m_pmx_free)
                            ts.inc(tensor.matmul(pMx, onc,
                                                 sqf[:, it2 * 512:(it2 + 1) * 512],
                                                 start=True, stop=True))
                        m_mmx = ts.n
                        m_sq_readers = ts.n

                        @blk.vector
                        def _(vector, it2=it2, m_mmx=m_mmx):
                            vector.wait_ge(ts.h, m_mmx)
                            vs.inc(vector.reduce_max(mxt[0:1, it2:it2 + 1], pMx, axis=AX.X))
                        m_pmx_free = vs.n

                    @blk.vector
                    def _(vector):
                        vector.wait_ge(vs.h, vs.n)
                        vs.inc(vector.reduce_max(mxs, mxt, axis=AX.X))
                    m_mxs = vs.n

                    @blk.scalar
                    def _(scalar, m_mxs=m_mxs):
                        scalar.wait_ge(vs.h, m_mxs)
                        ss.inc(scalar.activation(mexp, mxs, AF.Sqrt))
                    m_me = ss.n

                    @blk.vector
                    def _(vector, m_me=m_me):
                        vector.wait_ge(ss.h, m_me)
                        vs.inc(vector.tensor_scalar(mneg, mexp, -4.0, None, ALU.mult))
                    m_mn = vs.n

                    @blk.tensor
                    def _(tensor, m_mn=m_mn):
                        tensor.wait_ge(vs.h, m_mn)
                        tensor.wait_ge(ss.h, m_pbc_free)
                        ts.inc(tensor.matmul(pBc, onr, mneg, start=True, stop=True))
                    m_mb = ts.n

                    @blk.scalar
                    def _(scalar, pr=pr, m_mb=m_mb):
                        scalar.wait_ge(ts.h, m_mb)
                        ss.inc(scalar.copy(negM[:, pr:pr + 1], pBc))
                    m_pbc_free = ss.n

                    @blk.sync
                    def _(sync, pr=pr, m_sq=m_sq):
                        sync.wait_ge(vs.h, m_sq)
                        s_ktw.inc(sync.dma_start(out=kt_scr[pr], in_=ktp[pr % 2]), 16)

                m_negm_all = ss.n

                if debug:
                    @blk.sync
                    def _(sync):
                        s_ln1w.wait_all(sync)
                        sync.wait_ge(vs.h, vs.n)
                        sync.wait_ge(ss.h, ss.n)
                        sync.wait_ge(ts.h, ts.n)
                        s_dbg.inc(sync.dma_start(out=dbg["ln1t"], in_=ln1t_scr), 16)
                        s_dbg.inc(sync.dma_start(out=dbg["kt"], in_=kt_scr), 16)
                        s_dbg.inc(sync.dma_start(out=dbg["negm"], in_=negM), 16)
                        s_dbg.inc(sync.dma_start(out=dbg["ckt"], in_=ckT), 16)
                        s_dbg.inc(sync.dma_start(out=dbg["cv"], in_=cv_sb), 16)
                        sync.wait_ge(s_dbg.h, s_dbg.n)

            m_ph1_t, m_ph1_v, m_ph1_s = ts.n, vs.n, ss.n

            with contextlib.ExitStack() as ph2:
                def sb2(name, shp, dt=F32):
                    return ph2.enter_context(nc.sbuf_tensor(name, shp, dt))[:, :]
                lnsl = [sb2(f"lnsl{i}", [128, 512], BF16) for i in range(2)]
                qtp = sb2("qtp", [128, 512], BF16)
                qrA = sb2("qrA", [128, 512], BF16)
                qrB = sb2("qrB", [128, 512], BF16)
                NEXP = 6
                expb = [sb2(f"expb{i}", [128, 1024], BF16) for i in range(NEXP)]
                ts2 = [sb2(f"ts2_{i}", [128, 1024], BF16) for i in range(2)]
                t4 = [sb2(f"t4_{i}", [128, 1024], BF16) for i in range(2)]
                dacc = sb2("dacc", [128, H * 512])
                u_raw = sb2("u_raw", [128, H * 512], BF16)
                wvo_sb = sb2("wvo_sb", [128, H * 1024], BF16)
                dlgr = sb2("dlgr", [1, 1024])
                dinvr = sb2("dinvr", [1, 2048], BF16)
                xp2 = [sb2(f"xp2{i}", [128, D]) for i in range(2)]
                x1t = [sb2(f"x1t{i}", [128, D]) for i in range(2)]
                x1s = sb2("x1s", [128, D])
                stats2 = sb2("stats2", [128, 12])
                mv2 = sb2("mv2", [128, 2])
                lnv2 = sb2("lnv2", [128, 1])
                rstd2 = sb2("rstd2", [128, 1])
                ln2tile = [sb2(f"ln2tile{i}", [128, D], BF16) for i in range(2)]

                pSr = [ph2.enter_context(nc.psum_tensor(f"pSr{i}", [128, 1024], F32))[:, :]
                       for i in range(3)]
                pS = pSr
                pU = ph2.enter_context(nc.psum_tensor("pU", [128, 512], F32))[:, :]
                pU2 = ph2.enter_context(nc.psum_tensor("pU2", [128, 512], F32))[:, :]
                pSA = pSr[0]
                pSB = pSr[1]
                pQR = pSr[2][:, 0:512]
                pMS = pSr[2][:, 512:1024]

                @blk.sync
                def _(sync):
                    sync.wait_ge(ts.h, m_ph1_t)
                    sync.wait_ge(vs.h, m_ph1_v)
                    sync.wait_ge(ss.h, m_ph1_s)
                    sync.wait_ge(s_wvow.h, 16 * H)
                    s_wvo.inc(sync.dma_start(
                        out=wvo_sb[:, :].rearrange("p (h f) -> p h f", h=H),
                        in_=wvo_scr.rearrange("h p f -> p h f")), 16)
                m_wvold = s_wvo.n

                m_qklast = {-2: 0, -1: 0}
                m_exp_end = {-1: 0}
                m_qrope = {-1: 0}
                m_uev = {-1: 0}
                m_dacc_end = {}
                m_p4v = {-1: 0}
                m_expev = {}
                m_umm = {}
                m_dacc = {}
                m_attnev = {-1: 0}
                m_attnev_ps = {-1: 0}
                m_rstd2 = {-1: 0}
                m_norm2 = {-2: 0, -1: 0}
                m_dscale_prev = {}
                m_p4t = {-2: 0, -1: 0}
                m_yv = {}

                def issue_pair_loads(sync, gp):
                    ib_, pr_ = gp // NDC, gp % NDC
                    sync.wait_ge(ts.h, m_qklast[gp - 2])
                    if gp >= 2:
                        sync.wait_ge(s_ktw.h, 16 * NDC)
                        s_kt.issue(sync, ktp[gp % 2], kt_scr[pr_], gp - 2)
                    s_ln1r.issue(sync, lnsl[gp % 2],
                                 ln1t_scr[pr_][:, ib_ * 512:(ib_ + 1) * 512], gp)

                @blk.sync
                def _(sync):
                    s_ln1w.wait_all(sync)
                    issue_pair_loads(sync, 0)
                    if NIB * NDC > 1:
                        issue_pair_loads(sync, 1)

                m_qrot = {}

                def emit_qrot(gp):
                    bp = gp % 2

                    @blk.tensor
                    def _(tensor, gp=gp, bp=bp):
                        s_ln1r.wait(tensor, gp)
                        tensor.wait_ge(vs.h, m_qrope[gp - 1])
                        tensor.wait_ge(ss.h, m_exp_end[gp - 1])
                        ts.inc(tensor.matmul(pQR, rrotb, lnsl[bp],
                                             start=True, stop=True))
                    m_qrot[gp] = ts.n

                def emit_qrope(gp):
                    bp = gp % 2
                    ib_ = gp // NDC

                    @blk.vector
                    def _(vector, ib_=ib_, gp=gp, bp=bp):
                        vector.wait_ge(ts.h, m_qrot[gp])
                        vector.wait_ge(ts.h, m_qklast[gp - 1])
                        i0 = ib_ * 512
                        vs.inc(vector.tensor_tensor(qrA, lnsl[bp],
                                                    cosT[:, i0:i0 + 512], ALU.mult))
                        vs.inc(vector.tensor_tensor(qrB, pQR,
                                                    sinsT[:, i0:i0 + 512], ALU.mult))
                        vs.wait(vector)
                        vs.inc(vector.tensor_tensor(qtp, qrA, qrB, ALU.add))
                    m_qrope[gp] = vs.n

                TP = 2 * (T // 128) + 1
                VP = (T // 128) + 5

                def free_inc(vs_b, k):
                    g, r = k // 8, k % 8
                    return vs_b + 8 * g + (1, 1, 2, 2, 4, 4, 5, 5)[r]

                for ib in range(NIB):
                    ib_ts, ib_ss, ib_vs = ts.n, ss.n, vs.n

                    @blk.scalar
                    def _(scalar, ib=ib, ib_ts=ib_ts, ib_vs=ib_vs):
                        NJC = T // 128
                        for pr in range(NDC):
                            ts_b = ib_ts + pr * TP + 1
                            vs_b = ib_vs + pr * VP + 3
                            for k in range(NJC):
                                qkc = k + 1 if k < 2 else 2 * k - 1
                                scalar.wait_ge(ts.h, ts_b + qkc)
                                if k >= NEXP:
                                    scalar.wait_ge(vs.h, free_inc(vs_b, k - NEXP))
                                ss.inc(scalar.activation(expb[k % NEXP], pSr[k % 3],
                                                         AF.Exp,
                                                         bias=negM[:, pr:pr + 1],
                                                         scale=0.125))

                    for pr in range(NDC):
                        gp = ib * NDC + pr
                        bp = gp % 2
                        assert ts.n == ib_ts + pr * TP + (1 if pr else 0)
                        assert vs.n == ib_vs + (pr * VP + 3 if pr else 0)

                        if pr == 0:
                            emit_qrot(gp)
                            emit_qrope(gp)

                        NJC = T // 128
                        ts_b, vs_b = ts.n, vs.n
                        ss_b = ib_ss + pr * NJC

                        m_exp_end[gp] = ss_b + NJC

                        for k in range(NJC):
                            @blk.tensor
                            def _(tensor, gp=gp, bp=bp, k=k, ib=ib, pr=pr, ss_b=ss_b):
                                if k == 0:
                                    s_kt.wait(tensor, gp - 2)
                                    tensor.wait_ge(vs.h, m_qrope[gp])
                                    if gp == 0:
                                        tensor.wait_ge(ss.h, m_negm_all)
                                    if pr == 0 and ib > 0:
                                        tensor.wait_ge(vs.h, m_attnev_ps[(ib - 1) * 4 + 2])
                                if k == 1 and pr == 0 and ib > 0:
                                    tensor.wait_ge(vs.h, m_attnev_ps[(ib - 1) * 4 + 3])
                                if k == 2 and pr == 0 and ib > 0:
                                    tensor.wait_ge(vs.h, m_dscale_prev[ib - 1])
                                if k >= 3:
                                    tensor.wait_ge(ss.h, ss_b + k - 2)
                                for half in range(2):
                                    r0 = half * 64
                                    mm = tensor.matmul(
                                        pSr[k % 3][:, half * 512:(half + 1) * 512],
                                        ktp[bp][r0:r0 + 64, k * 128:(k + 1) * 128],
                                        qtp[r0:r0 + 64, :],
                                        start=True, stop=True,
                                        tile_position=(r0, 0))
                                ts.inc(mm)
                                if k >= 2:
                                    kk = k - 2
                                    tensor.wait_ge(ss.h, ss_b + kk + 1)
                                    if kk == 0:
                                        tensor.wait_ge(vs.h, m_uev[gp - 1])
                                        pass
                                    for half, pUx in ((0, pU), (1, pU2)):
                                        mm = tensor.matmul(
                                            pUx, cv_sb[:, kk * 128:(kk + 1) * 128],
                                            expb[kk % NEXP][:, half * 512:(half + 1) * 512],
                                            start=(kk == 0), stop=False)
                                    ts.inc(mm)

                        @blk.tensor
                        def _(tensor, gp=gp, ss_b=ss_b):
                            for kk in (NJC - 2, NJC - 1):
                                tensor.wait_ge(ss.h, ss_b + kk + 1)
                                for half, pUx in ((0, pU), (1, pU2)):
                                    mm = tensor.matmul(
                                        pUx, cv_sb[:, kk * 128:(kk + 1) * 128],
                                        expb[kk % NEXP][:, half * 512:(half + 1) * 512],
                                        start=(kk == 0), stop=(kk == NJC - 1))
                                ts.inc(mm)
                        assert ts.n == ts_b + 2 * NJC
                        m_qklast[gp] = ts.n

                        NG = NJC // 8

                        def dtree_l3(vector, pr, g8):
                            c0 = pr * 1024
                            vs.wait(vector)
                            vs.inc(vector.tensor_tensor(t4[0], t4[0], t4[1],
                                                        ALU.add))
                            vs.wait(vector)
                            if g8 == 0:
                                vs.inc(vector.tensor_copy(dacc[:, c0:c0 + 1024],
                                                          t4[0]))
                            else:
                                vs.inc(vector.tensor_tensor(
                                    dacc[:, c0:c0 + 1024], dacc[:, c0:c0 + 1024],
                                    t4[0], ALU.add))

                        for g8 in range(NG):
                            @blk.vector
                            def _(vector, pr=pr, g8=g8, ss_b=ss_b, gp=gp):
                                kb = 8 * g8
                                if debug and g8 == 0 and gp == 0:
                                    vector.wait_ge(s_dbg.h, s_dbg.n)
                                for hv in range(2):
                                    k0 = kb + 4 * hv
                                    vector.wait_ge(ss.h, ss_b + k0 + 2)
                                    vs.inc(vector.tensor_tensor(
                                        ts2[0], expb[k0 % NEXP], expb[(k0 + 1) % NEXP],
                                        ALU.add))
                                    vector.wait_ge(ss.h, ss_b + k0 + 4)
                                    vs.inc(vector.tensor_tensor(
                                        ts2[1], expb[(k0 + 2) % NEXP],
                                        expb[(k0 + 3) % NEXP], ALU.add))
                                    vs.wait(vector)
                                    vs.inc(vector.tensor_tensor(t4[hv], ts2[0], ts2[1],
                                                                ALU.add))
                                if g8 < NG - 1:
                                    dtree_l3(vector, pr, g8)
                        assert vs.n == vs_b + NJC - 2

                        if pr < NDC - 1:
                            emit_qrot(gp + 1)
                            emit_qrope(gp + 1)

                        @blk.vector
                        def _(vector, pr=pr, gp=gp):
                            vector.wait_ge(ts.h, ts.n)
                            vs.inc(vector.tensor_copy(
                                u_raw[:, (2 * pr) * 512:(2 * pr + 1) * 512], pU))
                            vs.inc(vector.tensor_copy(
                                u_raw[:, (2 * pr + 1) * 512:(2 * pr + 2) * 512], pU2))
                        m_uev[gp] = vs.n

                        @blk.vector
                        def _(vector, pr=pr):
                            dtree_l3(vector, pr, NG - 1)
                        m_dacc_end[gp] = vs.n

                        @blk.sync
                        def _(sync, gp=gp):
                            if 2 <= gp + 2 < NIB * NDC:
                                issue_pair_loads(sync, gp + 2)

                    m_pairs_done = m_dacc_end[ib * NDC + NDC - 1]

                    if debug:
                        @blk.sync
                        def _(sync, ib=ib):
                            sync.wait_ge(vs.h, m_pairs_done)
                            s_dbg.inc(sync.dma_start(out=dbg["dacc"][ib], in_=dacc), 16)
                            sync.wait_ge(s_dbg.h, s_dbg.n)

                    dt_b, ds_b, dv_b = ts.n, ss.n, vs.n

                    pDS4 = [pU[0:1, :], pU2[0:1, :], pU[32:33, :], pU2[32:33, :]]

                    @blk.scalar
                    def _(scalar, dt_b=dt_b):
                        for h in range(H):
                            pDS = pDS4[h % 4]
                            dsc = h + 1 if h < 2 else 2 * h
                            scalar.wait_ge(ts.h, dt_b + dsc)
                            lg = dlgr[0:1, (h % 2) * 512:(h % 2) * 512 + 512]
                            dv_ = dinvr[0:1, (h % 4) * 512:(h % 4) * 512 + 512]
                            ss.inc(scalar.activation(lg, pDS, AF.Ln))
                            ss.wait(scalar)
                            ss.inc(scalar.activation(dv_, lg, AF.Exp, scale=-1.0))
                    assert ss.n == ds_b + 2 * H

                    m_bc = {}

                    @blk.tensor
                    def _(tensor, ds_b=ds_b, dv_b=dv_b):
                        for h in range(H):
                            pDS = pDS4[h % 4]
                            if h < 2:
                                tensor.wait_ge(vs.h, m_pairs_done)
                            if h >= 4:
                                tensor.wait_ge(ss.h, ds_b + 2 * (h - 4) + 1)
                            ts.inc(tensor.matmul(pDS, oncf,
                                                 dacc[:, h * 512:(h + 1) * 512],
                                                 start=True, stop=True))
                            if h >= 1:
                                hb = h - 1
                                pBC = pQR if hb % 2 == 0 else pMS
                                tensor.wait_ge(ss.h, ds_b + 2 * hb + 2)
                                if hb >= 2:
                                    tensor.wait_ge(vs.h, dv_b + hb - 1)
                                ts.inc(tensor.matmul(pBC, onrb,
                                                     dinvr[0:1, (hb % 4) * 512:(hb % 4) * 512 + 512],
                                                     start=True, stop=True))
                                m_bc[hb] = ts.n
                        hb = H - 1
                        tensor.wait_ge(ss.h, ds_b + 2 * hb + 2)
                        tensor.wait_ge(vs.h, dv_b + hb - 1)
                        ts.inc(tensor.matmul(pMS, onrb,
                                             dinvr[0:1, (hb % 4) * 512:(hb % 4) * 512 + 512],
                                             start=True, stop=True))
                        m_bc[hb] = ts.n

                    m_dsc = {}

                    @blk.vector
                    def _(vector):
                        for h in range(H):
                            pBC = pQR if h % 2 == 0 else pMS
                            vector.wait_ge(ts.h, m_bc[h])
                            c0 = h * 512
                            vs.inc(vector.tensor_tensor(u_raw[:, c0:c0 + 512],
                                                        u_raw[:, c0:c0 + 512], pBC, ALU.mult))
                            m_dsc[h] = vs.n
                    assert vs.n == dv_b + H
                    m_dscale_all = vs.n
                    m_dscale_prev[ib] = vs.n

                    def emit_attn(t_, ib=ib):
                        git = ib * 4 + t_
                        pAT = pSA if t_ % 2 == 0 else pSB

                        @blk.sync
                        def _(sync, git=git):
                            if git >= 2:
                                sync.wait_ge(vs.h, m_attnev.get(git - 2, 0))
                            s_x2.issue(sync, xp2[git % 2],
                                       x_in[git * 128:(git + 1) * 128, :], git)

                        @blk.tensor
                        def _(tensor, t_=t_, git=git, pAT=pAT):
                            if t_ == 0:
                                tensor.wait_ge(s_wvo.h, m_wvold)
                            tensor.wait_ge(vs.h, m_attnev_ps.get(git - 2, 0))
                            for h in range(H):
                                if t_ == 0:
                                    tensor.wait_ge(vs.h, m_dsc[h])
                                for nt in range(2):
                                    mm = tensor.matmul(
                                        pAT[:, nt * 512:(nt + 1) * 512],
                                        u_raw[:, h * 512 + t_ * 128: h * 512 + (t_ + 1) * 128],
                                        wvo_sb[:, h * 1024 + nt * 512: h * 1024 + (nt + 1) * 512],
                                        start=(h == 0), stop=(h == H - 1))
                            ts.inc(mm)
                        m_attn = ts.n

                        @blk.vector
                        def _(vector, git=git, m_attn=m_attn, pAT=pAT):
                            vector.wait_ge(ts.h, m_attn)
                            vs.inc(vector.tensor_tensor(x1s, pAT, cbias_bc, ALU.add))
                            s_x2.wait(vector, git)
                            s_x1w.wait(vector, git - 2)
                            vector.wait_ge(vs.h, m_norm2.get(git - 2, 0))
                            vs.wait(vector)
                            vs.inc(vector.tensor_tensor(x1t[git % 2], x1s, xp2[git % 2],
                                                        ALU.add))
                        m_attnev[git] = vs.n
                        m_attnev_ps[git] = m_attnev[git] - 1

                    def emit_ln2(t_, ib=ib):
                        git = ib * 4 + t_
                        m_x1 = m_attnev[git]

                        @blk.sync
                        def _(sync, git=git, m_x1=m_x1):
                            sync.wait_ge(vs.h, m_x1)
                            s_x1w.issue(sync, x1_scr[git * 128:(git + 1) * 128, :],
                                        x1t[git % 2], git)

                        @blk.vector
                        def _(vector, git=git, m_x1=m_x1):
                            vector.wait_ge(vs.h, m_x1)
                            vector.wait_ge(ss.h, m_rstd2[git - 1])
                            vector.bn_stats(stats2[:, 0:6], x1t[git % 2][:, 0:512])
                            vs.inc(vector.bn_stats(stats2[:, 6:12], x1t[git % 2][:, 512:1024]))
                            vs.wait(vector)
                            vs.inc(vector.bn_aggr(mv2, stats2))
                        m_agg2 = vs.n

                        @blk.scalar
                        def _(scalar, git=git, m_agg2=m_agg2):
                            scalar.wait_ge(vs.h, m_agg2)
                            scalar.wait_ge(vs.h, m_norm2[git - 1])
                            ss.inc(scalar.activation(lnv2, mv2[:, 1:2], AF.Ln,
                                                     bias=epst, scale=1.0))
                            ss.wait(scalar)
                            ss.inc(scalar.activation(rstd2, lnv2, AF.Exp, scale=-0.5))
                        m_rstd2[git] = ss.n

                        @blk.vector
                        def _(vector, git=git):
                            vector.wait_ge(ss.h, m_rstd2[git])
                            s_ln2w.wait(vector, git - 2)
                            vs.inc(vector.tensor_scalar(ln2tile[git % 2], x1t[git % 2],
                                                        mv2[:, 0:1], rstd2,
                                                        ALU.subtract, ALU.mult))
                        m_norm2[git] = vs.n

                        @blk.sync
                        def _(sync, git=git):
                            sync.wait_ge(vs.h, m_norm2[git])
                            s_ln2w.issue(sync, ln2t_scr[git * 128:(git + 1) * 128, :],
                                         ln2tile[git % 2], git)
                        m_p4t[git] = m_norm2[git]


                    for t_ in range(4):
                        emit_attn(t_)
                        if t_ >= 1:
                            emit_ln2(t_ - 1)
                    emit_ln2(3)

                    m_p4v[ib] = m_norm2[ib * 4 + 3]

                if debug:
                    @blk.sync
                    def _(sync):
                        s_x1w.wait_all(sync)
                        s_ln2w.wait_all(sync)
                        sync.wait_ge(vs.h, vs.n)
                        s_dbg.inc(sync.dma_start(out=dbg["x1"], in_=x1_scr), 16)
                        s_dbg.inc(sync.dma_start(out=dbg["ln2t"], in_=ln2t_scr), 16)
                        sync.wait_ge(s_dbg.h, s_dbg.n)

            m_ph2_v, m_ph2_s, m_ph2_t = vs.n, ss.n, ts.n

        with contextlib.ExitStack() as ph3:
            def sb3(name, shp, dt=F32):
                return ph3.enter_context(nc.sbuf_tensor(name, shp, dt))[:, :]
            wt3_sb = sb3("wt3_sb", [128, NHT * D], BF16)
            hid = sb3("hid", [128, NHT * FIB], BF16)
            ln2t_ib = [sb3(f"ln2t_ib{i}", [128, NDC * FIB], BF16) for i in range(2)]
            acc = sb3("acc", [128, FIT * D])
            w12b = [sb3(f"w12b{i}", [128, 4096], BF16) for i in range(2)]
            sil = sb3("sil", [128, FIB], BF16)
            x1rb = [sb3(f"x1rb{i}", [128, D]) for i in range(2)]
            gt = sb3("gt", [128, NE])
            nmax1 = sb3("nmax1", [128, 1])
            gmax2 = sb3("gmax2", [128, 1])
            gdf = sb3("gdf", [128, NE])
            gmask = sb3("gmask", [128, NE])
            gexp = sb3("gexp", [128, NE])
            gme = sb3("gme", [128, NE])
            gsum = sb3("gsum", [128, 1])
            glog = sb3("glog", [128, 1])
            gden = sb3("gden", [128, 1])
            ytb = [sb3(f"ytb{i}", [128, D]) for i in range(2)]
            pFA = ph3.enter_context(nc.psum_tensor("pFA", [128, 2 * FIB], F32))[:, :]
            pFB = ph3.enter_context(nc.psum_tensor("pFB", [128, 2 * FIB], F32))[:, :]
            NN = FIB // 512

            @blk.tensor
            def _(tensor):
                tensor.wait_ge(vs.h, m_ph2_v)
                tensor.wait_ge(ss.h, m_ph2_s)

            @blk.vector
            def _(vector):
                vector.wait_ge(ts.h, m_ph2_t)
                vector.wait_ge(ss.h, m_ph2_s)

            @blk.scalar
            def _(scalar):
                scalar.wait_ge(ts.h, m_ph2_t)
                scalar.wait_ge(vs.h, m_ph2_v)

            m_hmm = {}
            m_silu = {}
            m_hid = {}
            m_eomm = {}
            m_accv = {}
            m_yv = {}
            m_w12_user = {}
            m_ln2ld = {}

            def w12_aps(e, wb):
                return (wt1_bf[e, 2 * wb:2 * wb + 2].rearrange("h c p f -> p h c f"),
                        wt2_bf[e, 2 * wb:2 * wb + 2].rearrange("h c p f -> p h c f"))

            def wait_w12(eng, gw):
                eng.wait_ge(s_w12.s[gw % 2].h, 32 * (gw // 2 + 1))

            def issue_ln2_load(sync, fb):
                for dc in range(NDC):
                    s_ln2r.inc(sync.dma_start_transpose(
                        ln2t_ib[fb % 2][:, dc * FIB:(dc + 1) * FIB],
                        ln2t_scr[fb * FIB:(fb + 1) * FIB, dc * 128:(dc + 1) * 128]), 16)
                m_ln2ld[fb] = s_ln2r.n

            for fb in range(NFB):
                @blk.sync
                def _(sync, fb=fb):
                    if fb == 0:
                        sync.wait_ge(ts.h, m_ph2_t)
                        sync.wait_ge(vs.h, m_ph2_v)
                        sync.wait_ge(ss.h, m_ph2_s)
                        s_ln2w.wait_all(sync)
                        issue_ln2_load(sync, 0)
                        if NFB > 1:
                            issue_ln2_load(sync, 1)
                    elif fb + 1 < NFB:
                        sync.wait_ge(ts.h, m_hmm[(fb - 1, NE - 1, NHT - 1)])
                        issue_ln2_load(sync, fb + 1)

                m_git = {}
                for it_ in range(FIT):
                    git = fb * FIT + it_

                    @blk.tensor
                    def _(tensor, fb=fb, it_=it_, git=git):
                        if it_ == 0:
                            tensor.wait_ge(s_ln2r.h, m_ln2ld[fb])
                            if fb > 0:
                                tensor.wait_ge(vs.h, m_accv.get(
                                    ((fb - 1) * NE + NE - 1, FIT - 2), 0))
                        if it_ > 0:
                            tensor.wait_ge(vs.h, m_git[git - 1] - 10)
                        for dc in range(NDC):
                            mm = tensor.matmul(
                                pFA[:, 0:NE],
                                ln2t_ib[fb % 2][:, dc * FIB + it_ * 128: dc * FIB + it_ * 128 + 128],
                                wgT_res[:, dc * NE:(dc + 1) * NE],
                                start=(dc == 0), stop=(dc == NDC - 1))
                        ts.inc(mm)
                    m_lg = ts.n

                    @blk.vector
                    def _(vector, m_lg=m_lg):
                        vector.wait_ge(ts.h, m_lg)
                        vs.inc(vector.tensor_copy(gt, pFA[:, 0:NE]))
                        vs.wait(vector)
                        vs.inc(vector.reduce_max(nmax1, gt, axis=AX.X, negate=True))
                        vs.wait(vector)
                        vs.inc(vector.tensor_scalar(gdf, gt, nmax1, None, ALU.add))
                        vs.wait(vector)
                        vs.inc(vector.tensor_scalar(gmask, gdf, 0.0, None, ALU.is_ge))
                        vs.wait(vector)
                        vs.inc(vector.scalar_tensor_tensor(gdf, gmask, -3.0e38, gt,
                                                           ALU.mult, ALU.add))
                        vs.wait(vector)
                        vs.inc(vector.reduce_max(gmax2, gdf, axis=AX.X))
                        vs.wait(vector)
                        vs.inc(vector.tensor_scalar(gmask, gt, gmax2, None, ALU.is_ge))
                    m_gv = vs.n

                    @blk.scalar
                    def _(scalar, m_gv=m_gv):
                        scalar.wait_ge(vs.h, m_gv)
                        ss.inc(scalar.activation(gexp, gt, AF.Exp, bias=nmax1, scale=1.0))
                    m_gexp = ss.n

                    @blk.vector
                    def _(vector, git=git, m_gexp=m_gexp):
                        vector.wait_ge(ss.h, m_gexp)
                        vs.inc(vector.tensor_tensor(gme, gexp, gmask, ALU.mult))
                        vs.wait(vector)
                        vs.inc(vector.reduce_sum(gsum, gme, axis=AX.X))
                        vs.wait(vector)
                        vs.inc(vector.reciprocal(gden, gsum))
                        vs.wait(vector)
                        vs.inc(vector.tensor_scalar(gbuf[:, git * NE:(git + 1) * NE],
                                                    gme, gden, None, ALU.mult))
                    m_git[git] = vs.n

                for e in range(NE):
                    ge = fb * NE + e

                    @blk.sync
                    def _(sync, e=e, ge=ge):
                        if ge == 0:
                            sync.wait_ge(g_w.h, m_wcast)
                        else:
                            sync.wait_ge(ts.h, m_eomm[(ge - 1, FIT - 1)])
                        s_w3.inc(sync.dma_start(
                            out=wt3_sb[:, :].rearrange("p (c f) -> p c f", c=NHT),
                            in_=wt3_bf[e].rearrange("c p f -> p c f")), 16)
                    m_w3ld = s_w3.n

                    hts_b, hss_b, hvs_b = ts.n, ss.n, vs.n
                    for ht in range(NHT):
                        m_hmm[(fb, e, ht)] = hts_b + ht + 1
                        m_silu[(ge, ht)] = hss_b + ht + 1
                        m_hid[(ge, ht)] = hvs_b + ht + 1
                    for wb in range(NW):
                        m_w12_user[ge * NW + wb] = hts_b + 2 * wb + 2

                    @blk.gpsimd
                    def _(gpsimd, e=e, ge=ge):
                        if ge == 0:
                            gpsimd.wait_ge(g_w.h, m_wcast)
                            gpsimd.wait_ge(ts.h, m_ph2_t)
                            gpsimd.wait_ge(vs.h, m_ph2_v)
                            gpsimd.wait_ge(ss.h, m_ph2_s)
                            s_ln2w.wait_all(gpsimd)
                            s_x1w.wait_all(gpsimd)
                            if debug:
                                gpsimd.wait_ge(s_dbg.h, s_dbg.n)
                        for wb in range(NW):
                            gw = ge * NW + wb
                            gpsimd.wait_ge(ts.h, m_w12_user.get(gw - 2, 0))
                            a1, a2 = w12_aps(e, wb)
                            sm = s_w12.s[gw % 2]
                            sm.inc(gpsimd.dma_start(
                                out=w12b[gw % 2][:, 0:2048]
                                .rearrange("p (h c f) -> p h c f", h=2, c=NDC), in_=a1), 16)
                            sm.inc(gpsimd.dma_start(
                                out=w12b[gw % 2][:, 2048:4096]
                                .rearrange("p (h c f) -> p h c f", h=2, c=NDC), in_=a2), 16)

                    @blk.scalar
                    def _(scalar, fb=fb, e=e, ge=ge, hts_b=hts_b, hvs_b=hvs_b):
                        for ht in range(NHT):
                            pF = pFA if (ht % 2 == 0) else pFB
                            scalar.wait_ge(ts.h, hts_b + ht + 1)
                            if ht > 0:
                                scalar.wait_ge(vs.h, hvs_b + ht)
                            ss.inc(scalar.activation(sil, pF[:, 0:FIB], AF.Silu))
                    assert ss.n == hss_b + NHT

                    for ht in range(NHT):
                        wb, hi = ht // 2, ht % 2
                        gw = ge * NW + wb
                        pF = pFA if (ht % 2 == 0) else pFB

                        @blk.tensor
                        def _(tensor, fb=fb, e=e, ge=ge, ht=ht, hi=hi, gw=gw, pF=pF,
                              hvs_b=hvs_b):
                            if ht == 0:
                                tensor.wait_ge(s_ln2r.h, m_ln2ld[fb])
                                tensor.wait_ge(vs.h, m_accv.get((ge - 1, FIT - 1), 0))
                                if e == 0:
                                    tensor.wait_ge(vs.h, m_git[fb * FIT + FIT - 1])
                            wait_w12(tensor, gw)
                            if ht >= 2:
                                tensor.wait_ge(vs.h, hvs_b + ht - 1)
                            base1 = hi * 1024
                            base2 = 2048 + hi * 1024
                            for dc in range(NDC):
                                for nn in range(NN):
                                    tensor.matmul(
                                        pF[:, nn * 512:(nn + 1) * 512],
                                        w12b[gw % 2][:, base1 + dc * 128: base1 + (dc + 1) * 128],
                                        ln2t_ib[fb % 2][:, dc * FIB + nn * 512: dc * FIB + (nn + 1) * 512],
                                        start=(dc == 0), stop=(dc == NDC - 1))
                            for dc in range(NDC):
                                for nn in range(NN):
                                    mm = tensor.matmul(
                                        pF[:, FIB + nn * 512: FIB + (nn + 1) * 512],
                                        w12b[gw % 2][:, base2 + dc * 128: base2 + (dc + 1) * 128],
                                        ln2t_ib[fb % 2][:, dc * FIB + nn * 512: dc * FIB + (nn + 1) * 512],
                                        start=(dc == 0), stop=(dc == NDC - 1))
                            ts.inc(mm)
                    assert ts.n == hts_b + NHT

                    @blk.vector
                    def _(vector, ge=ge, hss_b=hss_b):
                        for ht in range(NHT):
                            pF = pFA if (ht % 2 == 0) else pFB
                            vector.wait_ge(ss.h, hss_b + ht + 1)
                            vs.inc(vector.tensor_tensor(hid[:, ht * FIB:(ht + 1) * FIB],
                                                        sil, pF[:, FIB:2 * FIB], ALU.mult))
                    assert vs.n == hvs_b + NHT


                    for i128 in range(FIT):
                        pE = pFA[:, 0:1024] if (i128 % 2 == 0) else pFB[:, 0:1024]
                        git = fb * FIT + i128

                        @blk.tensor
                        def _(tensor, ge=ge, i128=i128, pE=pE):
                            if i128 == 0:
                                tensor.wait_ge(vs.h, m_hid[(ge, NHT - 1)])
                                tensor.wait_ge(s_w3.h, m_w3ld)
                            tensor.wait_ge(vs.h, m_accv.get((ge, i128 - 2), 0))
                            for hc in range(NHT):
                                for nn in range(2):
                                    mm = tensor.matmul(
                                        pE[:, nn * 512:(nn + 1) * 512],
                                        hid[:, hc * FIB + i128 * 128: hc * FIB + (i128 + 1) * 128],
                                        wt3_sb[:, hc * D + nn * 512: hc * D + (nn + 1) * 512],
                                        start=(hc == 0), stop=(hc == NHT - 1))
                            ts.inc(mm)
                        m_eomm[(ge, i128)] = ts.n

                        @blk.vector
                        def _(vector, fb=fb, e=e, ge=ge, i128=i128, pE=pE, git=git):
                            vector.wait_ge(ts.h, m_eomm[(ge, i128)])
                            gcol = gbuf[:, git * NE + e: git * NE + e + 1]
                            a0 = i128 * D
                            if e == 0:
                                vector.wait_ge(vs.h, m_yv.get(git - FIT, 0))
                                vs.inc(vector.tensor_scalar(acc[:, a0:a0 + D], pE,
                                                            gcol, None, ALU.mult))
                            else:
                                vs.inc(vector.scalar_tensor_tensor(acc[:, a0:a0 + D], pE,
                                                                   gcol, acc[:, a0:a0 + D],
                                                                   ALU.mult, ALU.add))
                        m_accv[(ge, i128)] = vs.n

                        if e == NE - 1:
                            @blk.sync
                            def _(sync, git=git):
                                if git == 0:
                                    s_x1w.wait_all(sync)
                                if git >= 2:
                                    sync.wait_ge(vs.h, m_yv.get(git - 2, 0))
                                s_x1r.issue(sync, x1rb[git % 2],
                                            x1_scr[git * 128:(git + 1) * 128, :], git)

                            @blk.vector
                            def _(vector, ge=ge, git=git, i128=i128):
                                s_x1r.wait(vector, git)
                                vector.wait_ge(vs.h, m_accv[(ge, i128)])
                                s_y.wait(vector, git - 2)
                                a0 = i128 * D
                                vs.inc(vector.tensor_tensor(ytb[git % 2], acc[:, a0:a0 + D],
                                                            x1rb[git % 2], ALU.add))
                            m_yv[git] = vs.n

                            @blk.sync
                            def _(sync, git=git):
                                sync.wait_ge(vs.h, m_yv[git])
                                s_y.issue(sync, y_out[git * 128:(git + 1) * 128, :],
                                          ytb[git % 2], git)

            if debug:
                @blk.sync
                def _(sync):
                    sync.wait_ge(vs.h, vs.n)
                    s_dbg.inc(sync.dma_start(
                        out=dbg["g"].rearrange("t p e -> p t e"),
                        in_=gbuf[:, :].rearrange("p (t e) -> p t e", t=NIT)), 16)
                    sync.wait_ge(s_dbg.h, s_dbg.n)

            @blk.sync
            def _(sync):
                s_y.wait_all(sync)

    return nc


def _rope_tables(T):
    d = np.arange(0, DH, 2, dtype=np.float64)
    inv = 1.0 / ROPE_BASE ** (d / DH)
    t = np.arange(T, dtype=np.float64)
    fr = t[:, None] * inv[None, :]
    emb = np.concatenate([fr, fr], axis=-1)
    cos = np.cos(emb).T
    sin = np.sin(emb).T
    sign = np.ones((DH, 1))
    sign[0:32] = -1.0
    sins = sign * sin
    cosP = np.concatenate([cos, cos], axis=0)
    sinsP = np.concatenate([sins, sins], axis=0)
    return (np.ascontiguousarray(cosP.astype(ml_dtypes.bfloat16)),
            np.ascontiguousarray(sinsP.astype(ml_dtypes.bfloat16)))


def _host_consts(T):
    ident = np.eye(128, dtype=ml_dtypes.bfloat16)
    rrot = np.zeros((128, 128), dtype=ml_dtypes.bfloat16)
    for m in range(128):
        rrot[m ^ 32, m] = 1.0
    cosP, sinsP = _rope_tables(T)
    return dict(ident=ident, rrot=rrot, cosT=cosP, sinsT=sinsP,
                onc=np.ones((128, 1), dtype=ml_dtypes.bfloat16),
                oncf=np.ones((128, 1), dtype=np.float32),
                onr=np.ones((1, 128), dtype=np.float32))


def _host_weights(ins):
    f = np.ascontiguousarray
    w1 = np.asarray(ins["L0_ffn_W1"])
    w2 = np.asarray(ins["L0_ffn_W2"])
    w3 = np.asarray(ins["L0_ffn_W3"])
    pad = HPAD - HID

    def tile_w12(w):
        wt = w.transpose(0, 2, 1)
        wt = np.pad(wt, ((0, 0), (0, 0), (0, pad)))
        wt = wt.reshape(NE, NDC, 128, NHT, 128)
        return f(wt.transpose(0, 3, 1, 2, 4))

    def tile_w3(w):
        wt = w.transpose(0, 2, 1)
        wt = np.pad(wt, ((0, 0), (0, pad), (0, 0)))
        return f(wt.reshape(NE, NHT, 128, D))

    return dict(
        w_dk=f(np.asarray(ins["L0_attn_w_dk"])),
        w_dv=f(np.asarray(ins["L0_attn_w_dv"])),
        w_uk=f(np.asarray(ins["L0_attn_w_uk"])),
        w_uvT=f(np.asarray(ins["L0_attn_w_uv"]).T),
        w_o=f(np.asarray(ins["L0_attn_w_o"])),
        b_dk=f(np.asarray(ins["L0_attn_b_dk"]).reshape(C, 1)),
        b_dv=f(np.asarray(ins["L0_attn_b_dv"]).reshape(C, 1)),
        b_uk=f(np.asarray(ins["L0_attn_b_uk"]).reshape(NDC, 128).T),
        b_uv=f(np.asarray(ins["L0_attn_b_uv"]).reshape(NDC, 128).T),
        b_o=f(np.asarray(ins["L0_attn_b_o"]).reshape(1, D)),
        wgT=f(np.asarray(ins["L0_ffn_Wg"]).T),
        w1t=tile_w12(w1), w2t=tile_w12(w2), w3t=tile_w3(w3),
    )


def kernel(**ins):
    x = np.asarray(ins["L0_attn_x"])
    Bx, T, Dx = x.shape
    nc = build_core(T=T, debug=False)
    common = {}
    common.update(_host_weights(ins))
    common.update(_host_consts(T))
    in_maps = []
    for b in range(Bx):
        m = dict(common)
        m["x"] = np.ascontiguousarray(x[b])
        in_maps.append(m)
    res = run_bass_kernel_spmd(nc, in_maps, core_ids=list(range(Bx)))
    return np.stack([np.asarray(r["y"]) for r in res.results], axis=0)
